# version 05
# baseline (speedup 1.0000x reference)
import contextlib
import math
import numpy as np
import ml_dtypes
import concourse.bass as bass
import concourse.mybir as mybir
from concourse.bass_utils import run_bass_kernel_spmd

F32 = mybir.dt.float32
BF16 = mybir.dt.bfloat16
I32 = mybir.dt.int32
I16 = mybir.dt.int16
U8 = mybir.dt.uint8
AF = mybir.ActivationFunctionType
ALU = mybir.AluOpType

D = 2048
NTOK = 2048
H = 16
HD = 128
CH = 128
NCH = NTOK // CH
MLP = 16384
GELU_A = 0.7978845608028654
GELU_B = GELU_A * 0.044715

ENGINES = ("sync", "vector", "scalar", "tensor", "gpsimd")
DEC_PARTS = {"load", "vchain", "mm", "evac", "out"}
FFN_STOP = None
B_LOADS = {"g", "u", "d"}
FFN_NCC = 64


class Sem:
    def __init__(self, handle):
        self.h = handle
        self.n = 0


class Builder:
    def __init__(self, nc, stage="full"):
        self.nc = nc
        self.stage = stage
        self.prog = {e: [] for e in ENGINES}
        self.sems = {}
        self.watermark = {e: {} for e in ENGINES}

    def em(self, eng, fn):
        self.prog[eng].append(fn)

    def op(self, eng, fn, inc=None, k=1):
        if inc is None:
            self.em(eng, fn)
            return None
        s = self.sems[inc]
        s.n += k
        self.em(eng, lambda e, fn=fn, s=s, k=k: fn(e).then_inc(s.h, k))
        return s.n

    def wait(self, eng, name, val=None):
        s = self.sems[name]
        if val is None:
            val = s.n
        if val <= 0:
            return
        wm = self.watermark[eng]
        if wm.get(name, 0) >= val:
            return
        wm[name] = val
        self.em(eng, lambda e, s=s, val=val: e.wait_ge(s.h, val))

    def sem_make(self, ctx, *names):
        for nm in names:
            self.sems[nm] = Sem(ctx.enter_context(self.nc.semaphore(nm)))


def _consts():
    idn = np.eye(128, dtype=ml_dtypes.bfloat16)
    ones1 = np.ones((1, 128), dtype=ml_dtypes.bfloat16)
    ones128 = np.ones((128, 128), dtype=ml_dtypes.bfloat16)
    iota_p = np.arange(128, dtype=np.float32).reshape(128, 1)
    iota_m128 = (np.arange(128, dtype=np.float32) - 128.0).reshape(128, 1)
    i = np.arange(128, dtype=np.float32)
    difft = np.maximum(i[None, :] - i[:, None], 0.0).astype(np.float32)
    maskt = (i[None, :] >= i[:, None]).astype(ml_dtypes.bfloat16)
    return idn, ones1, ones128, iota_p, iota_m128, difft, maskt


def build(stage="full", head_parts=None):
    nc = bass.Bass("TRN2", target_bir_lowering=False, debug=False)
    B = Builder(nc, stage)

    x_in = nc.dram_tensor("x", [NTOK, D], F32, kind="ExternalInput").ap()
    wqkv = nc.dram_tensor("wqkv", [D, 3 * D], F32, kind="ExternalInput").ap()
    bqkv = nc.dram_tensor("bqkv", [3 * D], F32, kind="ExternalInput").ap()
    wout = nc.dram_tensor("wout", [D, D], F32, kind="ExternalInput").ap()
    bout = nc.dram_tensor("bout", [D], F32, kind="ExternalInput").ap()
    slopes = nc.dram_tensor("slopes", [H], F32, kind="ExternalInput").ap()
    g_pk = nc.dram_tensor("g_pk", [MLP, D // 2], I32, kind="ExternalInput").ap()
    g_sc = nc.dram_tensor("g_sc", [MLP, D // 32], F32, kind="ExternalInput").ap()
    u_pk = nc.dram_tensor("u_pk", [MLP, D // 2], I32, kind="ExternalInput").ap()
    u_sc = nc.dram_tensor("u_sc", [MLP, D // 32], F32, kind="ExternalInput").ap()
    d_pk = nc.dram_tensor("d_pk", [D, MLP // 2], I32, kind="ExternalInput").ap()
    d_sc = nc.dram_tensor("d_sc", [D, MLP // 32], F32, kind="ExternalInput").ap()
    out = nc.dram_tensor("out", [NTOK, D], F32, kind="ExternalOutput").ap()

    qkv_sc = nc.dram_tensor("qkv_sc", [NTOK, 3 * D], BF16, kind="Internal").ap()
    wgT = nc.dram_tensor("wgT", [D, MLP], BF16, kind="Internal").ap()
    wuT = nc.dram_tensor("wuT", [D, MLP], BF16, kind="Internal").ap()
    wdT = nc.dram_tensor("wdT", [MLP, D], BF16, kind="Internal").ap()

    dbg = {}
    n_attn_heads = H
    if stage.startswith("attn"):
        n_attn_chunks = NCH if stage == "attn" else int(stage[4:])
    if stage.startswith("a") and "h" in stage and stage[1].isdigit():
        n_attn_chunks = int(stage[1:stage.index("h")])
        n_attn_heads = int(stage[stage.index("h") + 1:])
    if stage in ("qkv", "qkvd"):
        dbg["qkv"] = nc.dram_tensor("dbg_qkv", [NTOK, 3 * D], BF16, kind="ExternalOutput").ap()
    if stage.startswith("attn") or stage == "atd" or (stage.startswith("a") and "h" in stage and stage[1].isdigit()):
        dbg["attn"] = nc.dram_tensor("dbg_attn", [128, H * NTOK], BF16, kind="ExternalOutput").ap()
    if stage in ("ln1", "ln1d", "a4d"):
        dbg["x1"] = nc.dram_tensor("dbg_x1", [128, 16 * NTOK], BF16, kind="ExternalOutput").ap()
    if stage == "ffn0":
        dbg["x2"] = nc.dram_tensor("dbg_x2", [128, 8192], BF16, kind="ExternalOutput").ap()
        dbg["ih"] = nc.dram_tensor("dbg_ih", [128, 16 * NTOK], BF16, kind="ExternalOutput").ap()
        dbg["x1s"] = nc.dram_tensor("dbg_x1s", [128, 16 * NTOK], BF16, kind="ExternalOutput").ap()
    if stage == "decode":
        dbg["wg"] = nc.dram_tensor("dbg_wg", [D, MLP], BF16, kind="ExternalOutput").ap()
        dbg["wu"] = nc.dram_tensor("dbg_wu", [D, MLP], BF16, kind="ExternalOutput").ap()
        dbg["wd"] = nc.dram_tensor("dbg_wd", [MLP, D], BF16, kind="ExternalOutput").ap()

    idn_np, ones1_np, ones128_np, iota_p_np, iota_m128_np, difft_np, maskt_np = _consts()
    c_idn = nc.inline_tensor(idn_np, name="c_idn").ap()
    c_on1 = nc.inline_tensor(ones1_np, name="c_on1").ap()
    c_on128 = nc.inline_tensor(ones128_np, name="c_on128").ap()
    c_iop = nc.inline_tensor(iota_p_np, name="c_iop").ap()
    c_iom = nc.inline_tensor(iota_m128_np, name="c_iom").ap()
    c_diff = nc.inline_tensor(difft_np, name="c_diff").ap()
    c_mask = nc.inline_tensor(maskt_np, name="c_mask").ap()

    with contextlib.ExitStack() as ctx:
        sb = lambda name, shape, dt: ctx.enter_context(nc.sbuf_tensor(name, shape, dt))
        xT = sb("xT", [128, 16 * NTOK], BF16)
        aT = sb("aT", [128, 16 * NTOK], BF16)
        r3 = sb("r3", [128, 24576], U8)
        r4 = sb("r4", [128, 16384], U8)
        r5 = sb("r5", [128, 23040], U8)
        wd_sl = sb("wd_sl", [128, 3 * 1024], BF16)
        S_bf = sb("S_bf", [128, H * 128], BF16)
        decayT = sb("decayT", [128, H * 128], BF16)
        dtmp = sb("dtmp", [128, 128], BF16)
        idn = sb("idn", [128, 128], BF16)
        on128 = sb("on128", [128, 128], BF16)
        on1 = sb("on1", [1, 128], BF16)
        iop = sb("iop", [128, 1], F32)
        iom = sb("iom", [128, 1], F32)
        difft = None
        maskt = sb("maskt", [128, 128], BF16)
        sl_bc = sb("sl_bc", [128, H], F32)
        nsl = sb("nsl", [128, H], F32)
        n128s = sb("n128s", [128, H], F32)
        kd_col = sb("kd_col", [128, H], F32)
        qd_col = sb("qd_col", [128, H], F32)
        bd_col = sb("bd_col", [128, H], F32)
        bout_col = sb("bout_col", [128, 16], F32)
        qdT_sb = sb("qdT_sb", [128, 128], BF16)
        scTd = sb("scTd", [128, 128], BF16)
        kdb = sb("kdb", [128, 128], BF16)
        qT_sb = sb("qT_sb", [128, 128], BF16)[:]
        kT_sb = sb("kT_sb", [128, 128], BF16)[:]
        qdb_tok = dtmp[:]
        Stmp = maskt[:]

        PB = [ctx.enter_context(nc.psum_tensor(f"pb{i}", [128, 512], F32)) for i in range(8)]
        pbf = lambda i: PB[i][:].bitcast(BF16)
        pf = lambda i: PB[i][:]

        B.sem_make(ctx, *[f"{p}{i}" for p in ("ds", "do", "dq", "dp", "dwg", "dwu", "dwd", "dg", "du", "dd") for i in (0, 1)],
                   "dd2", "do2", "do3", "dc", "da", "db", "dz", "tM", "vM", "sM", "gM")
        block = ctx.enter_context(nc.Block())

        def view(region, off_b, dt, cols):
            esz = {F32: 4, BF16: 2, I32: 4, I16: 2, U8: 1}[dt]
            return region[:, off_b:off_b + cols * esz].bitcast(dt)

        difft = view(r4, 12288, F32, 128)

        for _t, _c in ((idn, c_idn), (on128, c_on128), (on1, c_on1), (iop, c_iop),
                       (iom, c_iom), (difft, c_diff), (maskt, c_mask)):
            B.op("sync", lambda e, _t=_t, _c=_c: e.dma_start(out=_t[:], in_=_c[:]), inc="dc", k=16)
        slo = bass.AP(slopes.tensor, slopes.offset, [[0, 128], [1, H]])
        B.op("sync", lambda e, slo=slo: e.dma_start(out=sl_bc[:], in_=slo), inc="dc", k=16)
        for cc in range(16):
            B.op("sync", lambda e, cc=cc: e.dma_start(
                out=bout_col[:, cc:cc + 1], in_=bout[cc * 128:(cc + 1) * 128][:, None]), inc="dc", k=16)

        B.wait("vector", "dc")
        B.op("vector", lambda e: e.tensor_scalar(out=nsl[:], in0=sl_bc[:], scalar1=-1.0, scalar2=None, op0=ALU.mult))
        B.op("vector", lambda e: e.tensor_scalar(out=n128s[:], in0=sl_bc[:], scalar1=-128.0, scalar2=None, op0=ALU.mult),
             inc="vM")
        B.wait("scalar", "vM")
        B.wait("scalar", "dc")
        for h in range(H):
            B.op("scalar", lambda e, h=h: e.activation(out=kd_col[:, h:h + 1], in_=iom[:], func=AF.Exp,
                                                       scale=sl_bc[:, h:h + 1]))
            B.op("scalar", lambda e, h=h: e.activation(out=qd_col[:, h:h + 1], in_=iop[:], func=AF.Exp,
                                                       scale=nsl[:, h:h + 1]))
        B.op("scalar", lambda e: e.activation(out=bd_col[:], in_=n128s[:], func=AF.Exp))
        for h in range(H):
            B.op("scalar", lambda e, h=h: e.activation(out=dtmp[:], in_=difft[:], func=AF.Exp,
                                                       scale=nsl[:, h:h + 1]), inc="sM")
            B.wait("vector", "sM")
            B.op("vector", lambda e, h=h: e.tensor_mul(out=decayT[:, h * 128:(h + 1) * 128], in0=dtmp[:], in1=maskt[:]),
                 inc="vM")
            B.wait("scalar", "vM")
        setup_v = B.sems["vM"].n

        PKb = [view(r5, 0, I32, 1024), view(r5, 4096, I32, 1024)]
        SCb = [view(r5, 8192, F32, 64), view(r5, 8448, F32, 64)]
        PK16 = [view(r5, 8704, I16, 512), view(r5, 9728, I16, 512)]
        SLR = [view(r5, 10752, I16, 512), view(r5, 11776, I16, 512)]
        SLC = [view(r5, 12800, I16, 512), view(r5, 13824, I16, 512)]
        WT = [view(r5, 14848, BF16, 2048), view(r5, 18944, BF16, 2048)]

        dec_state = {"i": 0}

        def decode_tiles():
            tiles = []
            for nt in range(128):
                tiles.append(("wg", g_pk[nt * 128:(nt + 1) * 128, :], g_sc[nt * 128:(nt + 1) * 128, :],
                              wgT, nt * 128, MLP))
            for nt in range(128):
                tiles.append(("wu", u_pk[nt * 128:(nt + 1) * 128, :], u_sc[nt * 128:(nt + 1) * 128, :],
                              wuT, nt * 128, MLP))
            for nch in range(8):
                for ht in range(16):
                    tiles.append(("wd", d_pk[ht * 128:(ht + 1) * 128, nch * 1024:(nch + 1) * 1024],
                                  d_sc[ht * 128:(ht + 1) * 128, nch * 64:(nch + 1) * 64],
                                  wdT, None, None, nch, ht))
            return tiles

        DEC_TILES = decode_tiles()
        dec_hist = []

        def pump_decode(n):
            if stage in ("qkv", "attn", "ln1") or stage.startswith("attn") or (stage.startswith("a") and "h" in stage and stage[1].isdigit()):
                return
            for _ in range(n):
                i = dec_state["i"]
                if i >= len(DEC_TILES):
                    return
                dec_state["i"] = i + 1
                t = DEC_TILES[i]
                name, pk_ap, sc_ap = t[0], t[1], t[2]
                wsem = {"wg": "dwg", "wu": "dwu", "wd": "dwd"}[name]
                ipar = i % 2
                rec = {}

                def emit_loads(j):
                    jpar = j % 2
                    jt = DEC_TILES[j]
                    if j >= 2:
                        B.wait("sync", "vM", dec_hist[j - 2]["v_done"])
                    B.op("sync", lambda e, pk_ap=jt[1], jpar=jpar: e.dma_start(out=PKb[jpar][:], in_=pk_ap),
                         inc=f"dp{jpar}", k=16)
                    B.op("sync", lambda e, sc_ap=jt[2], jpar=jpar: e.dma_start(out=SCb[jpar][:], in_=sc_ap),
                         inc=f"dp{jpar}", k=16)

                if "load" not in DEC_PARTS:
                    dec_hist.append({"v_done": B.sems["vM"].n, "evac": B.sems["sM"].n,
                                     "out_sem": f"{wsem}{ipar}", "out_dma": B.sems[f"{wsem}{ipar}"].n})
                    continue
                if i == 0:
                    emit_loads(0)
                    emit_loads(1)
                wt = WT[ipar]
                if "vchain" not in DEC_PARTS:
                    rec["v_done"] = B.sems["vM"].n
                    rec["evac"] = B.sems["sM"].n
                    rec["out_sem"] = f"{wsem}{ipar}"
                    rec["out_dma"] = B.sems[f"{wsem}{ipar}"].n
                    dec_hist.append(rec)
                    continue
                B.wait("vector", f"dp{ipar}")
                if i >= 2:
                    old = dec_hist[i - 2]
                    B.wait("vector", old["out_sem"], old["out_dma"])
                for half in range(2):
                    pkh = PK16[half]
                    B.op("vector", lambda e, ipar=ipar, half=half, pkh=pkh: e.tensor_copy(
                        out=pkh[:], in_=PKb[ipar][:, half * 512:(half + 1) * 512]))
                    scb = SCb[ipar][:, half * 32:(half + 1) * 32]
                    sc_bc = bass.AP(scb.tensor, scb.offset, [list(scb.ap[0]), list(scb.ap[1]), [0, 16]])
                    for si, strm in enumerate(("lo", "hi")):
                        R, C = SLR[si], SLC[si]
                        if strm == "lo":
                            B.op("vector", lambda e, pkh=pkh, R=R: e.tensor_scalar(
                                out=R[:], in0=pkh[:], scalar1=7, scalar2=None, op0=ALU.bitwise_and))
                        else:
                            B.op("vector", lambda e, pkh=pkh, R=R: e.tensor_scalar(
                                out=R[:], in0=pkh[:], scalar1=4, scalar2=7,
                                op0=ALU.logical_shift_right, op1=ALU.bitwise_and))
                        B.op("vector", lambda e, R=R, C=C: e.tensor_scalar(
                            out=C[:], in0=R[:], scalar1=16064.0, scalar2=16128.0, op0=ALU.mult, op1=ALU.min))
                        B.op("vector", lambda e, R=R, C=C: e.scalar_tensor_tensor(
                            out=C[:], in0=R[:], scalar=64.0, in1=C[:], op0=ALU.mult, op1=ALU.add))
                        if strm == "lo":
                            B.op("vector", lambda e, pkh=pkh, R=R: e.tensor_scalar(
                                out=R[:], in0=pkh[:], scalar1=8, scalar2=None, op0=ALU.bitwise_and))
                            sgn_mul = -4096.0
                        else:
                            B.op("vector", lambda e, pkh=pkh, R=R: e.tensor_scalar(
                                out=R[:], in0=pkh[:], scalar1=128, scalar2=None, op0=ALU.is_ge))
                            sgn_mul = -32768.0
                        B.op("vector", lambda e, R=R, C=C, sgn_mul=sgn_mul: e.scalar_tensor_tensor(
                            out=C[:], in0=R[:], scalar=sgn_mul, in1=C[:], op0=ALU.mult, op1=ALU.add))
                        o0 = half * 1024 + (0 if strm == "lo" else 1)
                        wslice = wt[:, o0:half * 1024 + 1024:2]
                        B.op("vector", lambda e, C=C, wslice=wslice, sc_bc=sc_bc: e.tensor_tensor(
                            out=wslice.rearrange("p (a b) -> p a b", b=16),
                            in0=C[:].bitcast(BF16).rearrange("p (a b) -> p a b", b=16),
                            in1=sc_bc, op=ALU.mult),
                            inc="vM")
                rec["v_done"] = B.sems["vM"].n
                if "mm" not in DEC_PARTS:
                    rec["evac"] = B.sems["sM"].n
                    rec["out_sem"] = f"{wsem}{ipar}"
                    rec["out_dma"] = B.sems[f"{wsem}{ipar}"].n
                    dec_hist.append(rec)
                    continue
                B.wait("tensor", "vM", rec["v_done"])
                def mm_round(rnd):
                    bank = 6 + rnd % 2
                    for jj in range(4):
                        j = rnd * 4 + jj
                        pslot = pf(bank)[:, jj * 128:(jj + 1) * 128]
                        B.op("tensor", lambda e, wt=wt, j=j, pslot=pslot: e.matmul(
                            pslot, wt[:, j * 128:(j + 1) * 128], idn[:], start=True, stop=True))

                def ev_round(rnd):
                    bank = 6 + rnd % 2
                    B.wait("scalar", "tM")
                    for jj in range(4):
                        j = rnd * 4 + jj
                        pslot = pf(bank)[:, jj * 128:(jj + 1) * 128]
                        B.op("scalar", lambda e, wt=wt, j=j, pslot=pslot: e.copy(
                            out=wt[:, j * 128:(j + 1) * 128], in_=pslot),
                            inc=("sM" if jj == 3 else None))

                B.wait("tensor", "sM")
                mm_round(0)
                mm_round(1)
                B.op("tensor", lambda e: e.drain(), inc="tM")
                ev_round(0)
                B.wait("tensor", "sM")
                mm_round(2)
                B.op("tensor", lambda e: e.drain(), inc="tM")
                ev_round(1)
                B.wait("tensor", "sM")
                mm_round(3)
                B.op("tensor", lambda e: e.drain(), inc="tM")
                ev_round(2)
                B.wait("tensor", "sM")
                for _d in range(3):
                    B.op("tensor", lambda e, _d=_d: e.matmul(
                        pf(6)[:, _d * 128:(_d + 1) * 128], idn[:], idn[:], start=True, stop=True))
                B.op("tensor", lambda e: e.drain(), inc="tM")
                ev_round(3)
                rec["evac"] = B.sems["sM"].n
                if "out" not in DEC_PARTS:
                    rec["out_sem"] = f"{wsem}{ipar}"
                    rec["out_dma"] = B.sems[f"{wsem}{ipar}"].n
                    dec_hist.append(rec)
                    continue
                if name in ("wg", "wu"):
                    tgt, n0 = (t[3], t[4])
                    oap = bass.AP(tgt.tensor, tgt.offset + n0, [[MLP, 128], [128 * MLP, 16], [1, 128]])
                else:
                    tgt, nch, ht = t[3], t[6], t[7]
                    oap = bass.AP(tgt.tensor, tgt.offset + (nch * 2048) * D + ht * 128,
                                  [[D, 128], [128 * D, 16], [1, 128]])
                B.wait("sync", "sM", rec["evac"])
                B.op("sync", lambda e, oap=oap, wt=wt: e.dma_start(out=oap, in_=wt[:]),
                     inc=f"{wsem}{ipar}", k=16)
                rec["out_sem"] = f"{wsem}{ipar}"
                rec["out_dma"] = B.sems[f"{wsem}{ipar}"].n
                dec_hist.append(rec)
                if i + 2 < len(DEC_TILES):
                    emit_loads(i + 2)

        if stage != "decode":
            xtokb = [view(r4, 0, BF16, 2048), view(r4, 4096, BF16, 2048)]
            xt_cons = [0, 0]
            for tc in range(16):
                tpar = tc % 2
                B.wait("gpsimd", "tM", xt_cons[tpar])
                B.op("gpsimd", lambda e, tc=tc, tpar=tpar: e.dma_start(
                    out=xtokb[tpar][:], in_=x_in[tc * 128:(tc + 1) * 128, :]), inc="da", k=16)
                B.wait("tensor", "da")
                for fc in range(16):
                    par = fc % 2
                    pslot = pf(2)[:, par * 128:(par + 1) * 128]
                    B.wait("tensor", "sM")
                    B.op("tensor", lambda e, fc=fc, tpar=tpar, pslot=pslot: e.matmul(
                        pslot, xtokb[tpar][:, fc * 128:(fc + 1) * 128], idn[:], start=True, stop=True))
                    B.op("tensor", lambda e: e.drain(), inc="tM")
                    B.wait("scalar", "tM")
                    B.op("scalar", lambda e, fc=fc, tc=tc, pslot=pslot: e.copy(
                        out=xT[:, fc * NTOK + tc * 128: fc * NTOK + (tc + 1) * 128], in_=pslot),
                        inc="sM")
                xt_cons[tpar] = B.sems["tM"].n
            a1_s = B.sems["sM"].n

        if stage != "decode":
            wq_pan = [view(r3, 0, BF16, 16 * 256), view(r3, 8192, BF16, 16 * 256)]
            bq_bf = [view(r4, 8192, BF16, 256)[0:1, :], view(r4, 8704, BF16, 256)[0:1, :]]
            qkvb = [view(r4, 4096 + 512 * i, BF16, 256) for i in range(4)]
            B.wait("tensor", "sM", a1_s)
            pan_consumed = [0, 0]
            qk_ps = [0, 0, 0, 0]
            for fgh in range(24):
                par = fgh % 2
                fcol = fgh * 256
                B.wait("gpsimd", "tM", pan_consumed[par])
                B.op("gpsimd", lambda e, fcol=fcol, par=par: e.dma_start(
                    out=wq_pan[par][:].rearrange("p (a b) -> p a b", b=256),
                    in_=bass.AP(wqkv.tensor, wqkv.offset + fcol, [[3 * D, 128], [128 * 3 * D, 16], [1, 256]])),
                    inc=f"ds{par}", k=16)
                B.op("gpsimd", lambda e, fcol=fcol, par=par: e.dma_start(
                    out=bq_bf[par], in_=bqkv[fcol:fcol + 256][None, :]), inc=f"ds{par}", k=16)
                B.wait("tensor", f"ds{par}")
                for tc in range(16):
                    qp = tc % 4
                    if qk_ps[qp]:
                        B.wait("tensor", "vM", qk_ps[qp])
                    for kc in range(16):
                        B.op("tensor", lambda e, kc=kc, tc=tc, par=par, qp=qp: e.matmul(
                            pf(qp)[:, 0:256],
                            xT[:, kc * NTOK + tc * 128: kc * NTOK + (tc + 1) * 128],
                            wq_pan[par][:, kc * 256:(kc + 1) * 256],
                            start=(kc == 0), stop=False))
                    B.op("tensor", lambda e, par=par, qp=qp: e.matmul(
                        pf(qp)[:, 0:256], on1[:], bq_bf[par], start=False, stop=True), inc="tM")
                    B.wait("vector", "tM")
                    B.wait("vector", f"do{qp}")
                    qk_ps[qp] = B.op("vector", lambda e, qp=qp: e.tensor_copy(out=qkvb[qp], in_=pf(qp)[:, 0:256]),
                                     inc="vM")
                    B.wait("sync", "vM")
                    B.op("sync", lambda e, tc=tc, fcol=fcol, qp=qp: e.dma_start(
                        out=qkv_sc[tc * 128:(tc + 1) * 128, fcol:fcol + 256], in_=qkvb[qp]), inc=f"do{qp}", k=16)
                    if tc % 3 == 2:
                        pump_decode(1)
                pan_consumed[par] = B.sems["tM"].n
                pump_decode(1)

        if stage in ("qkv", "qkvd"):
            if stage == "qkvd":
                pump_decode(len(DEC_TILES))
            for _d in ("do0", "do1", "do2", "do3"):
                B.wait("sync", _d)
            B.op("sync", lambda e: e.dma_start(out=dbg["qkv"][:, :], in_=qkv_sc[:, :]), inc="dz", k=16)
            _finish(B, block)
            return nc

        if stage != "decode":
            qkvc = [view(r3, 0, BF16, 6144), view(r3, 12288, BF16, 6144)]
            for _d in ("do0", "do1", "do2", "do3"):
                B.wait("sync", _d)
            for c in range(2):
                B.op("sync", lambda e, c=c: e.dma_start(out=qkvc[c % 2][:], in_=qkv_sc[c * 128:(c + 1) * 128, :]),
                     inc=f"dq{c % 2}", k=16)
            B.wait("vector", "sM")
            _is_bisect = stage.startswith("attn") or (stage.startswith("a") and "h" in stage and stage[1].isdigit())
            hbuf = [
                dict(qdb=qdb_tok, kdbb=kdb[:], qT=qT_sb, kT=kT_sb, qdT=qdT_sb[:], sc=scTd[:], St=Stmp,
                     ptr=pf(2), kq=pf(4)[:, 0:128], o=pf(5)[:, 0:128], S=pf(3)[:, 0:128]),
                dict(qdb=view(r4, 0, BF16, 128), kdbb=view(r4, 512, BF16, 128), qT=view(r4, 1024, BF16, 128),
                     kT=view(r4, 1536, BF16, 128), qdT=view(r4, 2048, BF16, 128), sc=view(r4, 2560, BF16, 128),
                     St=view(r4, 3072, BF16, 128),
                     ptr=pf(0), kq=pf(4)[:, 128:256], o=pf(5)[:, 128:256], S=pf(3)[:, 128:256]),
            ]
            mm_done = [0, 0]
            ev_done = [0, 0]
            sc_done = [0, 0]
            oev = [0, 0]
            su = [0, 0]
            for c in range(n_attn_chunks if _is_bisect else NCH):
                B.wait("scalar", f"dq{c % 2}")
                B.wait("tensor", f"dq{c % 2}")
                qc = qkvc[c % 2]
                for h in range(n_attn_heads if _is_bisect else H):
                    par = h % 2
                    hb = hbuf[par]
                    q_tok = qc[:, h * 128:(h + 1) * 128]
                    k_tok = qc[:, D + h * 128: D + (h + 1) * 128]
                    v_tok = qc[:, 2 * D + h * 128: 2 * D + (h + 1) * 128]
                    if mm_done[par]:
                        B.wait("scalar", "tM", mm_done[par])
                    B.op("scalar", lambda e, hb=hb, q_tok=q_tok, h=h: e.activation(
                        out=hb["qdb"], in_=q_tok, func=AF.Copy, scale=qd_col[:, h:h + 1]))
                    B.op("scalar", lambda e, hb=hb, k_tok=k_tok, h=h: e.activation(
                        out=hb["kdbb"], in_=k_tok, func=AF.Copy, scale=kd_col[:, h:h + 1]), inc="sM")
                    prep_s = B.sems["sM"].n
                    B.wait("tensor", "sM", prep_s)
                    B.op("tensor", lambda e, hb=hb, q_tok=q_tok: e.matmul(hb["ptr"][:, 0:128], q_tok, idn[:],
                                                                          start=True, stop=True))
                    B.op("tensor", lambda e, hb=hb, k_tok=k_tok: e.matmul(hb["ptr"][:, 128:256], k_tok, idn[:],
                                                                          start=True, stop=True))
                    B.op("tensor", lambda e, hb=hb: e.matmul(hb["ptr"][:, 256:384], hb["qdb"], idn[:],
                                                             start=True, stop=True))
                    tp_done = B.op("tensor", lambda e: e.drain(), inc="tM")
                    B.wait("scalar", "tM", tp_done)
                    B.op("scalar", lambda e, hb=hb: e.copy(out=hb["qT"], in_=hb["ptr"][:, 0:128]))
                    B.op("scalar", lambda e, hb=hb: e.copy(out=hb["kT"], in_=hb["ptr"][:, 128:256]))
                    ev_done[par] = B.op("scalar", lambda e, hb=hb: e.copy(out=hb["qdT"], in_=hb["ptr"][:, 256:384]),
                                        inc="sM")
                    B.wait("tensor", "sM", ev_done[par])
                    if sc_done[par]:
                        B.wait("tensor", "vM", sc_done[par])
                    B.op("tensor", lambda e, hb=hb: e.matmul(hb["kq"], hb["kT"], hb["qT"], start=True, stop=True))
                    kq_done = B.op("tensor", lambda e: e.drain(), inc="tM")
                    B.wait("vector", "tM", kq_done)
                    sc_done[par] = B.op("vector", lambda e, hb=hb, h=h: e.tensor_tensor(
                        out=hb["sc"], in0=hb["kq"], in1=decayT[:, h * 128:(h + 1) * 128], op=ALU.mult), inc="vM")
                    B.wait("tensor", "vM", sc_done[par])
                    if oev[par]:
                        B.wait("tensor", "sM", oev[par])
                    if su[par]:
                        B.wait("tensor", "vM", su[par])
                    B.op("tensor", lambda e, hb=hb, v_tok=v_tok, c=c: e.matmul(
                        hb["o"], v_tok, hb["sc"], start=True, stop=(c == 0)))
                    if c > 0:
                        B.op("tensor", lambda e, hb=hb, h=h: e.matmul(
                            hb["o"], S_bf[:, h * 128:(h + 1) * 128], hb["qdT"], start=False, stop=True))
                    B.op("tensor", lambda e, hb=hb, v_tok=v_tok: e.matmul(
                        hb["S"], hb["kdbb"], v_tok, start=True, stop=True))
                    mm_done[par] = B.op("tensor", lambda e: e.drain(), inc="tM")
                    B.wait("scalar", "tM", mm_done[par])
                    oev[par] = B.op("scalar", lambda e, hb=hb, h=h, c=c: e.copy(
                        out=aT[:, h * NTOK + c * 128: h * NTOK + (c + 1) * 128], in_=hb["o"]), inc="sM")
                    B.wait("vector", "tM", mm_done[par])
                    if c == 0:
                        su[par] = B.op("vector", lambda e, hb=hb, h=h: e.tensor_copy(
                            out=S_bf[:, h * 128:(h + 1) * 128], in_=hb["S"]), inc="vM")
                    else:
                        bdc = bd_col[:, h:h + 1]
                        bd_bc0 = bass.AP(bdc.tensor, bdc.offset, [list(bdc.ap[0]), [0, 128]])
                        B.op("vector", lambda e, hb=hb, h=h, bd_bc0=bd_bc0: e.tensor_tensor(
                            out=hb["St"], in0=S_bf[:, h * 128:(h + 1) * 128], in1=bd_bc0, op=ALU.mult))
                        su[par] = B.op("vector", lambda e, hb=hb, h=h: e.tensor_tensor(
                            out=S_bf[:, h * 128:(h + 1) * 128], in0=hb["St"], in1=hb["S"], op=ALU.add), inc="vM")
                    if h % 2 == 1:
                        pump_decode(1)
                if c + 2 < NCH:
                    B.wait("sync", "tM")
                    B.wait("sync", "sM")
                    B.op("sync", lambda e, c=c: e.dma_start(
                        out=qkvc[c % 2][:], in_=qkv_sc[(c + 2) * 128:(c + 3) * 128, :]), inc=f"dq{c % 2}", k=16)
                pump_decode(1)

        if stage.startswith("attn") or stage == "atd" or (stage.startswith("a") and "h" in stage and stage[1].isdigit()):
            if stage == "atd":
                pump_decode(len(DEC_TILES))
            B.wait("sync", "sM")
            B.op("sync", lambda e: e.dma_start(out=dbg["attn"][:, :], in_=aT[:]), inc="dz", k=16)
            _finish(B, block)
            return nc

        if stage != "decode":
            wo_pan = [view(r3, 0, BF16, 2048), view(r3, 4096, BF16, 2048)]
            tbuf = view(r4, 1024, BF16, 512)
            B.wait("tensor", "sM")
            B.wait("tensor", "vM")
            attn_done = B.sems["tM"].n
            pan_cons = [attn_done, attn_done]
            for cc in range(16):
                par = cc % 2
                B.wait("gpsimd", "tM", pan_cons[par])
                B.op("gpsimd", lambda e, cc=cc, par=par: e.dma_start(
                    out=wo_pan[par][:].rearrange("p (a b) -> p a b", b=128),
                    in_=bass.AP(wout.tensor, wout.offset + cc * 128, [[D, 128], [128 * D, 16], [1, 128]])),
                    inc=f"ds{par}", k=16)
                B.wait("tensor", f"ds{par}")
                for tg in range(4):
                    qp = tg % 2
                    B.wait("tensor", "vM")
                    for ac in range(16):
                        B.op("tensor", lambda e, ac=ac, tg=tg, par=par, qp=qp: e.matmul(
                            pf(qp)[:, 0:512],
                            wo_pan[par][:, ac * 128:(ac + 1) * 128],
                            aT[:, ac * NTOK + tg * 512: ac * NTOK + (tg + 1) * 512],
                            start=(ac == 0), stop=(ac == 15)), inc=("tM" if ac == 15 else None))
                    B.wait("vector", "tM")
                    B.op("vector", lambda e, cc=cc, qp=qp: e.tensor_scalar(
                        out=tbuf, in0=pf(qp)[:, 0:512], scalar1=bout_col[:, cc:cc + 1], scalar2=None, op0=ALU.add))
                    B.op("vector", lambda e, cc=cc, tg=tg: e.tensor_tensor(
                        out=xT[:, cc * NTOK + tg * 512: cc * NTOK + (tg + 1) * 512],
                        in0=tbuf,
                        in1=xT[:, cc * NTOK + tg * 512: cc * NTOK + (tg + 1) * 512], op=ALU.add), inc="vM")
                    pump_decode(1)
                pan_cons[par] = B.sems["tM"].n
                pump_decode(2)
            a4_done = B.sems["tM"].n

            if stage == "a4d":
                pump_decode(len(DEC_TILES))
                B.wait("sync", "vM")
                B.op("sync", lambda e: e.dma_start(out=dbg["x1"][:, :], in_=xT[:]), inc="dz", k=16)
                _finish(B, block)
                return nc
            _emit_ln(B, nc, xT, NTOK, view, r4, PB, pf, on128, pump=pump_decode)


        if stage in ("ln1", "ln1d"):
            if stage == "ln1d":
                pump_decode(len(DEC_TILES))
            B.wait("sync", "vM")
            B.op("sync", lambda e: e.dma_start(out=dbg["x1"][:, :], in_=xT[:]), inc="dz", k=16)
            _finish(B, block)
            return nc

        if stage == "decode":
            pump_decode(len(DEC_TILES))
            for _d in ("dwg0", "dwg1", "dwu0", "dwu1", "dwd0", "dwd1"):
                B.wait("sync", _d)
            for nm, tgt, src in (("wg", dbg["wg"], wgT), ("wu", dbg["wu"], wuT), ("wd", dbg["wd"], wdT)):
                B.op("sync", lambda e, tgt=tgt, src=src: e.dma_start(out=tgt[:, :], in_=src[:, :]), inc="dz", k=16)
            _finish(B, block)
            return nc

        pump_decode(len(DEC_TILES))

        g_sl = [view(r3, 0, BF16, 2048), view(r3, 4096, BF16, 2048)]
        u_sl = [view(r3, 8192, BF16, 2048), view(r3, 12288, BF16, 2048)]
        x2pre = view(r4, 0, BF16, 8192)
        g2b = view(r3, 16384, BF16, 512)
        bb = view(r3, 17408, BF16, 512)
        thb = view(r3, 18432, BF16, 512)
        hgb = view(r3, 19456, BF16, 512)
        p1b = view(r3, 20480, BF16, 512)
        gelb = view(r3, 21504, BF16, 512)
        gsb2 = view(r3, 22528, BF16, 512)
        usb2 = view(r3, 23552, BF16, 512)
        dn_ev = [view(r3, 16384 + i * 1024, BF16, 512) for i in range(4)]
        mb2 = view(r5, 9216, F32, 512)
        rb2 = view(r5, 11264, F32, 512)
        t1b = view(r5, 13312, F32, 512)
        sqb2 = [view(r5, 15360, BF16, 512), view(r5, 16384, BF16, 512)]
        nbuf = view(r5, 17408, BF16, 512)
        otok = view(r5, 18944, F32, 1024)

        slab_cons = {"g": [a4_done, a4_done], "u": [a4_done, a4_done]}
        wd_cons = [0, 0, 0]
        for _d in ("dwg0", "dwg1", "dwu0", "dwu1"):
            B.wait("sync", _d)

        for tq in range(1 if stage == "ffn0" else 4):
            t0 = tq * 512
            for nh in range(2):
                B.wait("tensor", "sM")
                B.wait("tensor", "vM")
                psfree_s = {}
                psfree_v = {}
                for ncc in range(FFN_NCC):
                    n_abs = nh * 64 + ncc
                    spar = ncc % 2
                    B.wait("sync", "tM", slab_cons["g"][spar])
                    B.op("sync", lambda e, n_abs=n_abs, spar=spar: e.dma_start(
                        out=g_sl[spar][:].rearrange("p (a b) -> p a b", b=128),
                        in_=bass.AP(wgT.tensor, wgT.offset + n_abs * 128,
                                    [[MLP, 128], [128 * MLP, 16], [1, 128]])), inc=f"dg{spar}", k=16)
                    B.wait("sync", "tM", slab_cons["u"][spar])
                    if "u" in B_LOADS:
                        B.op("sync", lambda e, n_abs=n_abs, spar=spar: e.dma_start(
                            out=u_sl[spar][:].rearrange("p (a b) -> p a b", b=128),
                            in_=bass.AP(wuT.tensor, wuT.offset + n_abs * 128,
                                        [[MLP, 128], [128 * MLP, 16], [1, 128]])), inc=f"du{spar}", k=16)
                    B.wait("tensor", f"dg{spar}")
                    B.wait("tensor", f"du{spar}")
                    gq = 0 if ncc % 2 == 0 else 2
                    uq = 1 if ncc % 2 == 0 else 3
                    if psfree_s.get(gq):
                        B.wait("tensor", "sM", psfree_s[gq])
                    if psfree_v.get(uq):
                        B.wait("tensor", "vM", psfree_v[uq])
                    for kc in range(16):
                        B.op("tensor", lambda e, kc=kc, spar=spar, t0=t0, gq=gq: e.matmul(
                            pf(gq)[:, 0:512], g_sl[spar][:, kc * 128:(kc + 1) * 128],
                            xT[:, kc * NTOK + t0: kc * NTOK + t0 + 512],
                            start=(kc == 0), stop=(kc == 15)), inc=("tM" if kc == 15 else None))
                    for kc in range(16):
                        B.op("tensor", lambda e, kc=kc, spar=spar, t0=t0, uq=uq: e.matmul(
                            pf(uq)[:, 0:512], u_sl[spar][:, kc * 128:(kc + 1) * 128],
                            xT[:, kc * NTOK + t0: kc * NTOK + t0 + 512],
                            start=(kc == 0), stop=(kc == 15)), inc=("tM" if kc == 15 else None))
                    slab_cons["g"][spar] = B.sems["tM"].n
                    slab_cons["u"][spar] = B.sems["tM"].n
                    mm_tgt = B.sems["tM"].n
                    B.wait("scalar", "tM", mm_tgt)
                    B.wait("scalar", "vM")
                    B.wait("scalar", "gM")
                    B.op("scalar", lambda e, gq=gq: e.copy(out=gsb2, in_=pf(gq)[:, 0:512]))
                    B.op("scalar", lambda e: e.activation(out=g2b, in_=gsb2, func=AF.Square), inc="sM")
                    s1 = B.sems["sM"].n
                    psfree_s[gq] = s1
                    B.wait("vector", "sM", s1)
                    B.op("vector", lambda e: e.tensor_scalar(out=g2b, in0=g2b, scalar1=GELU_B, scalar2=GELU_A,
                                                             op0=ALU.mult, op1=ALU.add))
                    B.op("vector", lambda e: e.tensor_mul(out=bb, in0=g2b, in1=gsb2))
                    B.op("vector", lambda e: e.tensor_scalar(out=hgb, in0=gsb2, scalar1=0.5, scalar2=None,
                                                             op0=ALU.mult), inc="vM")
                    v1 = B.sems["vM"].n
                    B.wait("scalar", "vM", v1)
                    B.op("scalar", lambda e: e.activation(out=thb, in_=bb, func=AF.Tanh), inc="sM")
                    s2 = B.sems["sM"].n
                    B.op("vector", lambda e, uq=uq: e.tensor_copy(out=usb2, in_=pf(uq)[:, 0:512]))
                    B.wait("vector", "sM", s2)
                    B.op("vector", lambda e: e.tensor_mul(out=p1b, in0=hgb, in1=thb))
                    B.op("vector", lambda e: e.tensor_add(out=gelb, in0=p1b, in1=hgb))
                    B.wait("vector", "tM")
                    psfree_v[uq] = B.op("vector", lambda e, ncc=ncc: e.tensor_mul(
                        out=aT[:, ncc * 512:(ncc + 1) * 512], in0=gelb, in1=usb2), inc="vM")
                if FFN_STOP == "gelu" and nh == 0:
                    break
                B.wait("tensor", "sM")
                B.wait("tensor", "vM")
                B.wait("sync", "dwd0")
                B.wait("sync", "dwd1")
                dn_free = [0, 0]
                for gen in range(4):
                    bgrp = (gen % 2) * 4
                    if dn_free[gen % 2]:
                        B.wait("tensor", "sM", dn_free[gen % 2])
                    for pr in range(32):
                        slot = pr % 3
                        B.wait("sync", "tM", wd_cons[slot])
                        B.op("sync", lambda e, pr=pr, nh=nh, gen=gen, slot=slot: e.dma_start(
                            out=wd_sl[:, slot * 1024:(slot + 1) * 1024].rearrange("p (a b) -> p a b", b=512),
                            in_=bass.AP(wdT.tensor, wdT.offset + (nh * 64 + pr * 2) * 128 * D + gen * 512,
                                        [[D, 128], [128 * D, 2], [1, 512]])), inc=f"dd{slot}", k=16)
                        B.wait("tensor", f"dd{slot}")
                        for a in range(2):
                            ncc = pr * 2 + a
                            for hc in range(4):
                                B.op("tensor", lambda e, a=a, hc=hc, ncc=ncc, slot=slot, bgrp=bgrp: e.matmul(
                                    pf(bgrp + hc)[:, 0:512],
                                    wd_sl[:, slot * 1024 + a * 512 + hc * 128: slot * 1024 + a * 512 + (hc + 1) * 128],
                                    aT[:, ncc * 512:(ncc + 1) * 512],
                                    start=(ncc == 0), stop=(ncc == 63)), inc=("tM" if hc == 3 else None))
                        wd_cons[slot] = B.sems["tM"].n
                    gen_tgt = B.sems["tM"].n
                    B.wait("scalar", "tM", gen_tgt)
                    B.wait("scalar", "vM")
                    for hc in range(4):
                        B.op("scalar", lambda e, hc=hc, bgrp=bgrp: e.copy(
                            out=dn_ev[hc], in_=pf(bgrp + hc)[:, 0:512]), inc=("sM" if hc == 3 else None))
                    dn_free[gen % 2] = B.sems["sM"].n
                    B.wait("vector", "sM", dn_free[gen % 2])
                    for hc in range(4):
                        ct = gen * 4 + hc
                        if nh == 0:
                            B.op("vector", lambda e, ct=ct, hc=hc, t0=t0: e.tensor_tensor(
                                out=x2pre[:, ct * 512:(ct + 1) * 512],
                                in0=dn_ev[hc],
                                in1=xT[:, ct * NTOK + t0: ct * NTOK + t0 + 512], op=ALU.add),
                                inc=("vM" if hc == 3 else None))
                        else:
                            B.op("vector", lambda e, ct=ct, hc=hc: e.tensor_tensor(
                                out=x2pre[:, ct * 512:(ct + 1) * 512],
                                in0=dn_ev[hc],
                                in1=x2pre[:, ct * 512:(ct + 1) * 512], op=ALU.add),
                                inc=("vM" if hc == 3 else None))

                if FFN_STOP == "down0" and nh == 0:
                    break
            if stage == "ffn0":
                B.wait("sync", "vM")
                B.op("sync", lambda e: e.dma_start(out=dbg["x2"][:, :], in_=x2pre), inc="dz", k=16)
                B.op("sync", lambda e: e.dma_start(out=dbg["ih"][:, :], in_=aT[:]), inc="dz", k=16)
                _finish(B, block)
                return nc
            _emit_ln2_and_out(B, nc, x2pre, PB, pf, pbf, on128, idn,
                              mb2, rb2, t1b, sqb2, nbuf, otok, out, t0)

        _finish(B, block)
    return nc


def _emit_ln(B, nc, xT, ntok, view, r4, PB, pf, on128, pump=None):
    mb = view(r4, 0, F32, 512)
    rb = view(r4, 2048, F32, 512)
    t1 = view(r4, 4096, F32, 512)
    sqb = [view(r4, 6144, BF16, 512), view(r4, 7168, BF16, 512)]
    dbuf = view(r4, 8192, BF16, 512)
    B.wait("scalar", "vM")
    for tg in range(4):
        c0 = tg * 512
        for cc in range(16):
            sp = cc % 2
            B.wait("scalar", "tM")
            B.op("scalar", lambda e, cc=cc, c0=c0, sp=sp: e.activation(
                out=sqb[sp], in_=xT[:, cc * ntok + c0: cc * ntok + c0 + 512], func=AF.Square), inc="sM")
            B.wait("tensor", "sM")
            if cc == 0:
                B.wait("tensor", "vM")
            B.op("tensor", lambda e, cc=cc, c0=c0: e.matmul(
                pf(0)[:, 0:512], on128[:], xT[:, cc * ntok + c0: cc * ntok + c0 + 512],
                start=(cc == 0), stop=(cc == 15)))
            B.op("tensor", lambda e, cc=cc, sp=sp: e.matmul(
                pf(1)[:, 0:512], on128[:], sqb[sp],
                start=(cc == 0), stop=(cc == 15)), inc="tM")
        stats_tgt = B.sems["tM"].n
        B.wait("vector", "tM", stats_tgt)
        B.op("vector", lambda e: e.tensor_scalar(out=mb, in0=pf(0)[:, 0:512], scalar1=1.0 / 2048, scalar2=None,
                                                 op0=ALU.mult))
        B.op("vector", lambda e: e.tensor_scalar(out=t1, in0=pf(1)[:, 0:512], scalar1=1.0 / 2048, scalar2=None,
                                                 op0=ALU.mult))
        B.op("vector", lambda e: e.tensor_mul(out=rb, in0=mb, in1=mb))
        B.op("vector", lambda e: e.tensor_sub(out=t1, in0=t1, in1=rb))
        B.op("vector", lambda e: e.tensor_scalar(out=t1, in0=t1, scalar1=1e-5, scalar2=None, op0=ALU.add), inc="vM")
        B.wait("scalar", "vM")
        B.op("scalar", lambda e: e.activation(out=t1, in_=t1, func=AF.Ln))
        B.op("scalar", lambda e: e.activation(out=rb, in_=t1, func=AF.Exp, scale=-0.5), inc="sM")
        B.wait("vector", "sM")
        for cc in range(16):
            B.op("vector", lambda e, cc=cc, c0=c0: e.tensor_sub(
                out=dbuf, in0=xT[:, cc * ntok + c0: cc * ntok + c0 + 512], in1=mb))
            B.op("vector", lambda e, cc=cc, c0=c0: e.tensor_mul(
                out=xT[:, cc * ntok + c0: cc * ntok + c0 + 512], in0=dbuf, in1=rb),
                inc=("vM" if cc == 15 else None))
        if pump is not None:
            pump(6)


def _emit_ln2_and_out(B, nc, x2pre, PB, pf, pbf, on128, idn, mb, rb, t1, sqb, nbuf, otok, out, t0):
    for cc in range(16):
        sp = cc % 2
        B.wait("scalar", "tM")
        B.wait("scalar", "vM")
        B.op("scalar", lambda e, cc=cc, sp=sp: e.activation(
            out=sqb[sp], in_=x2pre[:, cc * 512:(cc + 1) * 512], func=AF.Square), inc="sM")
        B.wait("tensor", "sM")
        if cc == 0:
            B.wait("tensor", "vM")
        B.op("tensor", lambda e, cc=cc: e.matmul(
            pf(0)[:, 0:512], on128[:], x2pre[:, cc * 512:(cc + 1) * 512],
            start=(cc == 0), stop=(cc == 15)))
        B.op("tensor", lambda e, cc=cc, sp=sp: e.matmul(
            pf(1)[:, 0:512], on128[:], sqb[sp],
            start=(cc == 0), stop=(cc == 15)), inc="tM")
    stats_tgt = B.sems["tM"].n
    B.wait("vector", "tM", stats_tgt)
    B.op("vector", lambda e: e.tensor_scalar(out=mb, in0=pf(0)[:, 0:512], scalar1=1.0 / 2048, scalar2=None,
                                             op0=ALU.mult))
    B.op("vector", lambda e: e.tensor_scalar(out=t1, in0=pf(1)[:, 0:512], scalar1=1.0 / 2048, scalar2=None,
                                             op0=ALU.mult))
    B.op("vector", lambda e: e.tensor_mul(out=rb, in0=mb, in1=mb))
    B.op("vector", lambda e: e.tensor_sub(out=t1, in0=t1, in1=rb))
    B.op("vector", lambda e: e.tensor_scalar(out=t1, in0=t1, scalar1=1e-5, scalar2=None, op0=ALU.add), inc="vM")
    B.wait("scalar", "vM")
    B.op("scalar", lambda e: e.activation(out=t1, in_=t1, func=AF.Ln))
    B.op("scalar", lambda e: e.activation(out=rb, in_=t1, func=AF.Exp, scale=-0.5), inc="sM")
    for tsub in range(4):
        for half in range(2):
            for ci in range(8):
                cc = half * 8 + ci
                B.wait("vector", "sM")
                B.wait("vector", "tM")
                B.op("vector", lambda e, cc=cc: e.tensor_sub(
                    out=nbuf, in0=x2pre[:, cc * 512:(cc + 1) * 512], in1=mb))
                B.op("vector", lambda e: e.tensor_mul(out=nbuf, in0=nbuf, in1=rb), inc="vM")
                B.wait("tensor", "vM")
                B.op("tensor", lambda e, tsub=tsub: e.matmul(
                    pf(2)[:, 0:128], nbuf[:, tsub * 128:(tsub + 1) * 128], idn[:], start=True, stop=True))
                B.op("tensor", lambda e: e.drain(), inc="tM")
                B.wait("vector", "tM")
                B.wait("vector", "dz")
                B.op("vector", lambda e, ci=ci: e.tensor_copy(out=otok[:, ci * 128:(ci + 1) * 128],
                                                              in_=pf(2)[:, 0:128]), inc="vM")
            B.wait("sync", "vM")
            B.op("sync", lambda e, tsub=tsub, half=half, t0=t0: e.dma_start(
                out=out[t0 + tsub * 128: t0 + (tsub + 1) * 128, half * 1024:(half + 1) * 1024],
                in_=otok[:]), inc="dz", k=16)


def _finish(B, block):
    for _nm in list(B.sems):
        if _nm[0] == "d":
            B.wait("sync", _nm)
    for eng in ENGINES:
        thunks = B.prog[eng]
        dec = getattr(block, eng)

        def runner(e, thunks=thunks):
            for f in thunks:
                f(e)
        dec(runner)


_NC_CACHE = {}


def _get_nc(stage):
    if stage not in _NC_CACHE:
        _NC_CACHE[stage] = build(stage)
    return _NC_CACHE[stage]


def kernel(L0_attn_x, L0_attn_Wqkv, L0_attn_bqkv, L0_attn_Wout, L0_attn_bout, L0_attn_slopes,
           L0_ffn_gate_W_packed, L0_ffn_gate_scales, L0_ffn_up_W_packed, L0_ffn_up_scales,
           L0_ffn_down_W_packed, L0_ffn_down_scales):
    nc = _get_nc("full")
    x = np.ascontiguousarray(np.asarray(L0_attn_x, dtype=np.float32))
    common = {
        "wqkv": np.ascontiguousarray(np.asarray(L0_attn_Wqkv, dtype=np.float32)),
        "bqkv": np.ascontiguousarray(np.asarray(L0_attn_bqkv, dtype=np.float32)),
        "wout": np.ascontiguousarray(np.asarray(L0_attn_Wout, dtype=np.float32)),
        "bout": np.ascontiguousarray(np.asarray(L0_attn_bout, dtype=np.float32)),
        "slopes": np.ascontiguousarray(np.asarray(L0_attn_slopes, dtype=np.float32)),
        "g_pk": np.ascontiguousarray(np.asarray(L0_ffn_gate_W_packed, dtype=np.int32)),
        "g_sc": np.ascontiguousarray(np.asarray(L0_ffn_gate_scales, dtype=np.float32)),
        "u_pk": np.ascontiguousarray(np.asarray(L0_ffn_up_W_packed, dtype=np.int32)),
        "u_sc": np.ascontiguousarray(np.asarray(L0_ffn_up_scales, dtype=np.float32)),
        "d_pk": np.ascontiguousarray(np.asarray(L0_ffn_down_W_packed, dtype=np.int32)),
        "d_sc": np.ascontiguousarray(np.asarray(L0_ffn_down_scales, dtype=np.float32)),
    }
    in_maps = [dict(common, x=np.ascontiguousarray(x[b])) for b in range(8)]
    res = run_bass_kernel_spmd(nc, in_maps, core_ids=list(range(8)))
    return np.stack([r["out"] for r in res.results], axis=0)


# version 06
# speedup vs baseline: 1.1243x; 1.1243x over previous
import contextlib
import math
import numpy as np
import ml_dtypes
import concourse.bass as bass
import concourse.mybir as mybir
from concourse.bass_utils import run_bass_kernel_spmd

F32 = mybir.dt.float32
BF16 = mybir.dt.bfloat16
I32 = mybir.dt.int32
I16 = mybir.dt.int16
U8 = mybir.dt.uint8
AF = mybir.ActivationFunctionType
ALU = mybir.AluOpType

D = 2048
NTOK = 2048
H = 16
HD = 128
CH = 128
NCH = NTOK // CH
MLP = 16384
GELU_A = 0.7978845608028654
GELU_B = GELU_A * 0.044715

ENGINES = ("sync", "vector", "scalar", "tensor", "gpsimd")
DEC_PARTS = {"load", "vchain", "mm", "evac", "out"}
FFN_STOP = None
B_LOADS = {"g", "u", "d"}
FFN_NCC = 64


class Sem:
    def __init__(self, handle):
        self.h = handle
        self.n = 0


class Builder:
    def __init__(self, nc, stage="full"):
        self.nc = nc
        self.stage = stage
        self.prog = {e: [] for e in ENGINES}
        self.sems = {}
        self.watermark = {e: {} for e in ENGINES}

    def em(self, eng, fn):
        self.prog[eng].append(fn)

    def op(self, eng, fn, inc=None, k=1):
        if inc is None:
            self.em(eng, fn)
            return None
        s = self.sems[inc]
        s.n += k
        self.em(eng, lambda e, fn=fn, s=s, k=k: fn(e).then_inc(s.h, k))
        return s.n

    def wait(self, eng, name, val=None):
        s = self.sems[name]
        if val is None:
            val = s.n
        if val <= 0:
            return
        wm = self.watermark[eng]
        if wm.get(name, 0) >= val:
            return
        wm[name] = val
        self.em(eng, lambda e, s=s, val=val: e.wait_ge(s.h, val))

    def sem_make(self, ctx, *names):
        for nm in names:
            self.sems[nm] = Sem(ctx.enter_context(self.nc.semaphore(nm)))


def _consts():
    idn = np.eye(128, dtype=ml_dtypes.bfloat16)
    ones1 = np.ones((1, 128), dtype=ml_dtypes.bfloat16)
    ones128 = np.ones((128, 128), dtype=ml_dtypes.bfloat16)
    iota_p = np.arange(128, dtype=np.float32).reshape(128, 1)
    iota_m128 = (np.arange(128, dtype=np.float32) - 128.0).reshape(128, 1)
    i = np.arange(128, dtype=np.float32)
    difft = np.maximum(i[None, :] - i[:, None], 0.0).astype(np.float32)
    maskt = (i[None, :] >= i[:, None]).astype(ml_dtypes.bfloat16)
    return idn, ones1, ones128, iota_p, iota_m128, difft, maskt


def build(stage="full", head_parts=None):
    nc = bass.Bass("TRN2", target_bir_lowering=False, debug=False)
    B = Builder(nc, stage)

    x_in = nc.dram_tensor("x", [NTOK, D], F32, kind="ExternalInput").ap()
    wqkv = nc.dram_tensor("wqkv", [D, 3 * D], F32, kind="ExternalInput").ap()
    bqkv = nc.dram_tensor("bqkv", [3 * D], F32, kind="ExternalInput").ap()
    wout = nc.dram_tensor("wout", [D, D], F32, kind="ExternalInput").ap()
    bout = nc.dram_tensor("bout", [D], F32, kind="ExternalInput").ap()
    slopes = nc.dram_tensor("slopes", [H], F32, kind="ExternalInput").ap()
    g_pk = nc.dram_tensor("g_pk", [MLP, D // 2], I32, kind="ExternalInput").ap()
    g_sc = nc.dram_tensor("g_sc", [MLP, D // 32], F32, kind="ExternalInput").ap()
    u_pk = nc.dram_tensor("u_pk", [MLP, D // 2], I32, kind="ExternalInput").ap()
    u_sc = nc.dram_tensor("u_sc", [MLP, D // 32], F32, kind="ExternalInput").ap()
    d_pk = nc.dram_tensor("d_pk", [D, MLP // 2], I32, kind="ExternalInput").ap()
    d_sc = nc.dram_tensor("d_sc", [D, MLP // 32], F32, kind="ExternalInput").ap()
    out = nc.dram_tensor("out", [NTOK, D], F32, kind="ExternalOutput").ap()

    qkv_sc = nc.dram_tensor("qkv_sc", [NTOK, 3 * D], BF16, kind="Internal").ap()
    wgT = nc.dram_tensor("wgT", [D, MLP], BF16, kind="Internal").ap()
    wuT = nc.dram_tensor("wuT", [D, MLP], BF16, kind="Internal").ap()
    wdT = nc.dram_tensor("wdT", [MLP, D], BF16, kind="Internal").ap()

    dbg = {}
    n_attn_heads = H
    if stage.startswith("attn"):
        n_attn_chunks = NCH if stage == "attn" else int(stage[4:])
    if stage.startswith("a") and "h" in stage and stage[1].isdigit():
        n_attn_chunks = int(stage[1:stage.index("h")])
        n_attn_heads = int(stage[stage.index("h") + 1:])
    if stage in ("qkv", "qkvd"):
        dbg["qkv"] = nc.dram_tensor("dbg_qkv", [NTOK, 3 * D], BF16, kind="ExternalOutput").ap()
    if stage.startswith("attn") or stage == "atd" or (stage.startswith("a") and "h" in stage and stage[1].isdigit()):
        dbg["attn"] = nc.dram_tensor("dbg_attn", [128, H * NTOK], BF16, kind="ExternalOutput").ap()
    if stage in ("ln1", "ln1d", "a4d"):
        dbg["x1"] = nc.dram_tensor("dbg_x1", [128, 16 * NTOK], BF16, kind="ExternalOutput").ap()
    if stage == "ffn0":
        dbg["x2"] = nc.dram_tensor("dbg_x2", [128, 8192], BF16, kind="ExternalOutput").ap()
        dbg["ih"] = nc.dram_tensor("dbg_ih", [128, 16 * NTOK], BF16, kind="ExternalOutput").ap()
        dbg["x1s"] = nc.dram_tensor("dbg_x1s", [128, 16 * NTOK], BF16, kind="ExternalOutput").ap()
    if stage == "decode":
        dbg["wg"] = nc.dram_tensor("dbg_wg", [D, MLP], BF16, kind="ExternalOutput").ap()
        dbg["wu"] = nc.dram_tensor("dbg_wu", [D, MLP], BF16, kind="ExternalOutput").ap()
        dbg["wd"] = nc.dram_tensor("dbg_wd", [MLP, D], BF16, kind="ExternalOutput").ap()

    idn_np, ones1_np, ones128_np, iota_p_np, iota_m128_np, difft_np, maskt_np = _consts()
    c_idn = nc.inline_tensor(idn_np, name="c_idn").ap()
    c_on1 = nc.inline_tensor(ones1_np, name="c_on1").ap()
    c_on128 = nc.inline_tensor(ones128_np, name="c_on128").ap()
    c_iop = nc.inline_tensor(iota_p_np, name="c_iop").ap()
    c_iom = nc.inline_tensor(iota_m128_np, name="c_iom").ap()
    c_diff = nc.inline_tensor(difft_np, name="c_diff").ap()
    c_mask = nc.inline_tensor(maskt_np, name="c_mask").ap()

    with contextlib.ExitStack() as ctx:
        sb = lambda name, shape, dt: ctx.enter_context(nc.sbuf_tensor(name, shape, dt))
        xT = sb("xT", [128, 16 * NTOK], BF16)
        aT = sb("aT", [128, 16 * NTOK], BF16)
        r3 = sb("r3", [128, 24576], U8)
        r4 = sb("r4", [128, 16384], U8)
        r5 = sb("r5", [128, 23040], U8)
        wd_sl = sb("wd_sl", [128, 3 * 1024], BF16)
        S_bf = sb("S_bf", [128, H * 128], BF16)
        decayT = sb("decayT", [128, H * 128], BF16)
        dtmp = sb("dtmp", [128, 128], BF16)
        idn = sb("idn", [128, 128], BF16)
        on128 = sb("on128", [128, 128], BF16)
        on1 = sb("on1", [1, 128], BF16)
        iop = sb("iop", [128, 1], F32)
        iom = sb("iom", [128, 1], F32)
        difft = None
        maskt = sb("maskt", [128, 128], BF16)
        sl_bc = sb("sl_bc", [128, H], F32)
        nsl = sb("nsl", [128, H], F32)
        n128s = sb("n128s", [128, H], F32)
        kd_col = sb("kd_col", [128, H], F32)
        qd_col = sb("qd_col", [128, H], F32)
        bd_col = sb("bd_col", [128, H], F32)
        bout_col = sb("bout_col", [128, 16], F32)
        qdT_sb = sb("qdT_sb", [128, 128], BF16)
        scTd = sb("scTd", [128, 128], BF16)
        kdb = sb("kdb", [128, 128], BF16)
        qT_sb = sb("qT_sb", [128, 128], BF16)[:]
        kT_sb = sb("kT_sb", [128, 128], BF16)[:]
        qdb_tok = dtmp[:]
        Stmp = maskt[:]

        PB = [ctx.enter_context(nc.psum_tensor(f"pb{i}", [128, 512], F32)) for i in range(8)]
        pbf = lambda i: PB[i][:].bitcast(BF16)
        pf = lambda i: PB[i][:]

        B.sem_make(ctx, *[f"{p}{i}" for p in ("ds", "do", "dq", "dp", "dwg", "dwu", "dwd", "dg", "du", "dd") for i in (0, 1)],
                   "dd2", "do2", "do3", "dc", "da", "db", "dz", "tM", "vM", "sM", "gM")
        block = ctx.enter_context(nc.Block())

        def view(region, off_b, dt, cols):
            esz = {F32: 4, BF16: 2, I32: 4, I16: 2, U8: 1}[dt]
            return region[:, off_b:off_b + cols * esz].bitcast(dt)

        difft = view(r4, 12288, F32, 128)

        for _t, _c in ((idn, c_idn), (on128, c_on128), (on1, c_on1), (iop, c_iop),
                       (iom, c_iom), (difft, c_diff), (maskt, c_mask)):
            B.op("sync", lambda e, _t=_t, _c=_c: e.dma_start(out=_t[:], in_=_c[:]), inc="dc", k=16)
        slo = bass.AP(slopes.tensor, slopes.offset, [[0, 128], [1, H]])
        B.op("sync", lambda e, slo=slo: e.dma_start(out=sl_bc[:], in_=slo), inc="dc", k=16)
        for cc in range(16):
            B.op("sync", lambda e, cc=cc: e.dma_start(
                out=bout_col[:, cc:cc + 1], in_=bout[cc * 128:(cc + 1) * 128][:, None]), inc="dc", k=16)

        B.wait("vector", "dc")
        B.op("vector", lambda e: e.tensor_scalar(out=nsl[:], in0=sl_bc[:], scalar1=-1.0, scalar2=None, op0=ALU.mult))
        B.op("vector", lambda e: e.tensor_scalar(out=n128s[:], in0=sl_bc[:], scalar1=-128.0, scalar2=None, op0=ALU.mult),
             inc="vM")
        B.wait("scalar", "vM")
        B.wait("scalar", "dc")
        for h in range(H):
            B.op("scalar", lambda e, h=h: e.activation(out=kd_col[:, h:h + 1], in_=iom[:], func=AF.Exp,
                                                       scale=sl_bc[:, h:h + 1]))
            B.op("scalar", lambda e, h=h: e.activation(out=qd_col[:, h:h + 1], in_=iop[:], func=AF.Exp,
                                                       scale=nsl[:, h:h + 1]))
        B.op("scalar", lambda e: e.activation(out=bd_col[:], in_=n128s[:], func=AF.Exp))
        for h in range(H):
            B.op("scalar", lambda e, h=h: e.activation(out=dtmp[:], in_=difft[:], func=AF.Exp,
                                                       scale=nsl[:, h:h + 1]), inc="sM")
            B.wait("vector", "sM")
            B.op("vector", lambda e, h=h: e.tensor_mul(out=decayT[:, h * 128:(h + 1) * 128], in0=dtmp[:], in1=maskt[:]),
                 inc="vM")
            B.wait("scalar", "vM")
        setup_v = B.sems["vM"].n

        PKb = [view(r5, 0, I32, 1024), view(r5, 4096, I32, 1024)]
        SCb = [view(r5, 8192, F32, 64), view(r5, 8448, F32, 64)]
        PK16 = [view(r5, 8704, I16, 512), view(r5, 9728, I16, 512)]
        SLR = [view(r5, 10752, I16, 512), view(r5, 11776, I16, 512)]
        SLC = [view(r5, 12800, I16, 512), view(r5, 13824, I16, 512)]
        WT = [view(r5, 14848, BF16, 2048), view(r5, 18944, BF16, 2048)]

        dec_state = {"i": 0}

        def decode_tiles():
            tiles = []
            for nt in range(128):
                tiles.append(("wg", g_pk[nt * 128:(nt + 1) * 128, :], g_sc[nt * 128:(nt + 1) * 128, :],
                              wgT, nt * 128, MLP))
            for nt in range(128):
                tiles.append(("wu", u_pk[nt * 128:(nt + 1) * 128, :], u_sc[nt * 128:(nt + 1) * 128, :],
                              wuT, nt * 128, MLP))
            for nch in range(8):
                for ht in range(16):
                    tiles.append(("wd", d_pk[ht * 128:(ht + 1) * 128, nch * 1024:(nch + 1) * 1024],
                                  d_sc[ht * 128:(ht + 1) * 128, nch * 64:(nch + 1) * 64],
                                  wdT, None, None, nch, ht))
            return tiles

        DEC_TILES = decode_tiles()
        dec_hist = []

        def pump_decode(n):
            if stage in ("qkv", "attn", "ln1") or stage.startswith("attn") or (stage.startswith("a") and "h" in stage and stage[1].isdigit()):
                return
            for _ in range(n):
                i = dec_state["i"]
                if i >= len(DEC_TILES):
                    return
                dec_state["i"] = i + 1
                t = DEC_TILES[i]
                name, pk_ap, sc_ap = t[0], t[1], t[2]
                wsem = {"wg": "dwg", "wu": "dwu", "wd": "dwd"}[name]
                ipar = i % 2
                rec = {}

                def emit_loads(j):
                    jpar = j % 2
                    jt = DEC_TILES[j]
                    if j >= 2:
                        B.wait("sync", "vM", dec_hist[j - 2]["v_done"])
                    B.op("sync", lambda e, pk_ap=jt[1], jpar=jpar: e.dma_start(out=PKb[jpar][:], in_=pk_ap),
                         inc=f"dp{jpar}", k=16)
                    B.op("sync", lambda e, sc_ap=jt[2], jpar=jpar: e.dma_start(out=SCb[jpar][:], in_=sc_ap),
                         inc=f"dp{jpar}", k=16)

                if "load" not in DEC_PARTS:
                    dec_hist.append({"v_done": B.sems["vM"].n, "evac": B.sems["sM"].n,
                                     "out_sem": f"{wsem}{ipar}", "out_dma": B.sems[f"{wsem}{ipar}"].n})
                    continue
                if i == 0:
                    emit_loads(0)
                    emit_loads(1)
                wt = WT[ipar]
                if "vchain" not in DEC_PARTS:
                    rec["v_done"] = B.sems["vM"].n
                    rec["evac"] = B.sems["sM"].n
                    rec["out_sem"] = f"{wsem}{ipar}"
                    rec["out_dma"] = B.sems[f"{wsem}{ipar}"].n
                    dec_hist.append(rec)
                    continue
                B.wait("vector", f"dp{ipar}")
                if i >= 2:
                    old = dec_hist[i - 2]
                    B.wait("vector", old["out_sem"], old["out_dma"])
                for half in range(2):
                    pkh = PK16[half]
                    B.op("vector", lambda e, ipar=ipar, half=half, pkh=pkh: e.tensor_copy(
                        out=pkh[:], in_=PKb[ipar][:, half * 512:(half + 1) * 512]))
                    scb = SCb[ipar][:, half * 32:(half + 1) * 32]
                    sc_bc = bass.AP(scb.tensor, scb.offset, [list(scb.ap[0]), list(scb.ap[1]), [0, 16]])
                    for si, strm in enumerate(("lo", "hi")):
                        R, C = SLR[si], SLC[si]
                        if strm == "lo":
                            B.op("vector", lambda e, pkh=pkh, R=R: e.tensor_scalar(
                                out=R[:], in0=pkh[:], scalar1=7, scalar2=None, op0=ALU.bitwise_and))
                        else:
                            B.op("vector", lambda e, pkh=pkh, R=R: e.tensor_scalar(
                                out=R[:], in0=pkh[:], scalar1=4, scalar2=7,
                                op0=ALU.logical_shift_right, op1=ALU.bitwise_and))
                        B.op("vector", lambda e, R=R, C=C: e.tensor_scalar(
                            out=C[:], in0=R[:], scalar1=16064.0, scalar2=16128.0, op0=ALU.mult, op1=ALU.min))
                        B.op("vector", lambda e, R=R, C=C: e.scalar_tensor_tensor(
                            out=C[:], in0=R[:], scalar=64.0, in1=C[:], op0=ALU.mult, op1=ALU.add))
                        if strm == "lo":
                            B.op("vector", lambda e, pkh=pkh, R=R: e.tensor_scalar(
                                out=R[:], in0=pkh[:], scalar1=8, scalar2=None, op0=ALU.bitwise_and))
                            sgn_mul = -4096.0
                        else:
                            B.op("vector", lambda e, pkh=pkh, R=R: e.tensor_scalar(
                                out=R[:], in0=pkh[:], scalar1=128, scalar2=None, op0=ALU.is_ge))
                            sgn_mul = -32768.0
                        B.op("vector", lambda e, R=R, C=C, sgn_mul=sgn_mul: e.scalar_tensor_tensor(
                            out=C[:], in0=R[:], scalar=sgn_mul, in1=C[:], op0=ALU.mult, op1=ALU.add))
                        o0 = half * 1024 + (0 if strm == "lo" else 1)
                        wslice = wt[:, o0:half * 1024 + 1024:2]
                        B.op("vector", lambda e, C=C, wslice=wslice, sc_bc=sc_bc: e.tensor_tensor(
                            out=wslice.rearrange("p (a b) -> p a b", b=16),
                            in0=C[:].bitcast(BF16).rearrange("p (a b) -> p a b", b=16),
                            in1=sc_bc, op=ALU.mult),
                            inc="vM")
                rec["v_done"] = B.sems["vM"].n
                if "mm" not in DEC_PARTS:
                    rec["evac"] = B.sems["sM"].n
                    rec["out_sem"] = f"{wsem}{ipar}"
                    rec["out_dma"] = B.sems[f"{wsem}{ipar}"].n
                    dec_hist.append(rec)
                    continue
                B.wait("tensor", "vM", rec["v_done"])
                def mm_round(rnd):
                    bank = 6 + rnd % 2
                    for jj in range(4):
                        j = rnd * 4 + jj
                        pslot = pf(bank)[:, jj * 128:(jj + 1) * 128]
                        B.op("tensor", lambda e, wt=wt, j=j, pslot=pslot: e.matmul(
                            pslot, wt[:, j * 128:(j + 1) * 128], idn[:], start=True, stop=True))

                def ev_round(rnd):
                    bank = 6 + rnd % 2
                    B.wait("scalar", "tM")
                    for jj in range(4):
                        j = rnd * 4 + jj
                        pslot = pf(bank)[:, jj * 128:(jj + 1) * 128]
                        B.op("scalar", lambda e, wt=wt, j=j, pslot=pslot: e.copy(
                            out=wt[:, j * 128:(j + 1) * 128], in_=pslot),
                            inc=("sM" if jj == 3 else None))

                B.wait("tensor", "sM")
                mm_round(0)
                mm_round(1)
                B.op("tensor", lambda e: e.drain(), inc="tM")
                ev_round(0)
                B.wait("tensor", "sM")
                mm_round(2)
                B.op("tensor", lambda e: e.drain(), inc="tM")
                ev_round(1)
                B.wait("tensor", "sM")
                mm_round(3)
                B.op("tensor", lambda e: e.drain(), inc="tM")
                ev_round(2)
                B.wait("tensor", "sM")
                for _d in range(3):
                    B.op("tensor", lambda e, _d=_d: e.matmul(
                        pf(6)[:, _d * 128:(_d + 1) * 128], idn[:], idn[:], start=True, stop=True))
                B.op("tensor", lambda e: e.drain(), inc="tM")
                ev_round(3)
                rec["evac"] = B.sems["sM"].n
                if "out" not in DEC_PARTS:
                    rec["out_sem"] = f"{wsem}{ipar}"
                    rec["out_dma"] = B.sems[f"{wsem}{ipar}"].n
                    dec_hist.append(rec)
                    continue
                if name in ("wg", "wu"):
                    tgt, n0 = (t[3], t[4])
                    oap = bass.AP(tgt.tensor, tgt.offset + n0, [[MLP, 128], [128 * MLP, 16], [1, 128]])
                else:
                    tgt, nch, ht = t[3], t[6], t[7]
                    oap = bass.AP(tgt.tensor, tgt.offset + (nch * 2048) * D + ht * 128,
                                  [[D, 128], [128 * D, 16], [1, 128]])
                B.wait("sync", "sM", rec["evac"])
                B.op("sync", lambda e, oap=oap, wt=wt: e.dma_start(out=oap, in_=wt[:]),
                     inc=f"{wsem}{ipar}", k=16)
                rec["out_sem"] = f"{wsem}{ipar}"
                rec["out_dma"] = B.sems[f"{wsem}{ipar}"].n
                dec_hist.append(rec)
                if i + 2 < len(DEC_TILES):
                    emit_loads(i + 2)

        if stage != "decode":
            xtokb = [view(r4, 0, BF16, 2048), view(r4, 4096, BF16, 2048)]
            xt_cons = [0, 0]
            xa_sem = ["da", "db"]
            for tc in range(16):
                tpar = tc % 2
                B.wait("gpsimd", "tM", xt_cons[tpar])
                B.op("gpsimd", lambda e, tc=tc, tpar=tpar: e.dma_start(
                    out=xtokb[tpar][:], in_=x_in[tc * 128:(tc + 1) * 128, :]), inc=xa_sem[tpar], k=16)
                B.wait("tensor", xa_sem[tpar])
                for fc in range(16):
                    par = fc % 2
                    pslot = pf(2)[:, par * 128:(par + 1) * 128]
                    B.wait("tensor", "sM")
                    B.op("tensor", lambda e, fc=fc, tpar=tpar, pslot=pslot: e.matmul(
                        pslot, xtokb[tpar][:, fc * 128:(fc + 1) * 128], idn[:], start=True, stop=True))
                    B.op("tensor", lambda e: e.drain(), inc="tM")
                    B.wait("scalar", "tM")
                    B.op("scalar", lambda e, fc=fc, tc=tc, pslot=pslot: e.copy(
                        out=xT[:, fc * NTOK + tc * 128: fc * NTOK + (tc + 1) * 128], in_=pslot),
                        inc="sM")
                xt_cons[tpar] = B.sems["tM"].n
            a1_s = B.sems["sM"].n

        if stage != "decode":
            wq_pan = [view(r3, 0, BF16, 16 * 256), view(r3, 8192, BF16, 16 * 256)]
            bq_bf = [view(r4, 8192, BF16, 256)[0:1, :], view(r4, 8704, BF16, 256)[0:1, :]]
            qkvb = [view(r4, 4096 + 512 * i, BF16, 256) for i in range(4)]
            B.wait("tensor", "sM", a1_s)
            pan_consumed = [0, 0]
            qk_ps = [0, 0, 0, 0]
            for fgh in range(24):
                par = fgh % 2
                fcol = fgh * 256
                B.wait("gpsimd", "tM", pan_consumed[par])
                B.op("gpsimd", lambda e, fcol=fcol, par=par: e.dma_start(
                    out=wq_pan[par][:].rearrange("p (a b) -> p a b", b=256),
                    in_=bass.AP(wqkv.tensor, wqkv.offset + fcol, [[3 * D, 128], [128 * 3 * D, 16], [1, 256]])),
                    inc=f"ds{par}", k=16)
                B.op("gpsimd", lambda e, fcol=fcol, par=par: e.dma_start(
                    out=bq_bf[par], in_=bqkv[fcol:fcol + 256][None, :]), inc=f"ds{par}", k=16)
                B.wait("tensor", f"ds{par}")
                for tc in range(16):
                    qp = tc % 4
                    if qk_ps[qp]:
                        B.wait("tensor", "vM", qk_ps[qp])
                    for kc in range(16):
                        B.op("tensor", lambda e, kc=kc, tc=tc, par=par, qp=qp: e.matmul(
                            pf(qp)[:, 0:256],
                            xT[:, kc * NTOK + tc * 128: kc * NTOK + (tc + 1) * 128],
                            wq_pan[par][:, kc * 256:(kc + 1) * 256],
                            start=(kc == 0), stop=False))
                    B.op("tensor", lambda e, par=par, qp=qp: e.matmul(
                        pf(qp)[:, 0:256], on1[:], bq_bf[par], start=False, stop=True), inc="tM")
                    B.wait("vector", "tM")
                    B.wait("vector", f"do{qp}")
                    qk_ps[qp] = B.op("vector", lambda e, qp=qp: e.tensor_copy(out=qkvb[qp], in_=pf(qp)[:, 0:256]),
                                     inc="vM")
                    B.wait("sync", "vM")
                    B.op("sync", lambda e, tc=tc, fcol=fcol, qp=qp: e.dma_start(
                        out=qkv_sc[tc * 128:(tc + 1) * 128, fcol:fcol + 256], in_=qkvb[qp]), inc=f"do{qp}", k=16)
                pan_consumed[par] = B.sems["tM"].n
                pump_decode(6)

        if stage in ("qkv", "qkvd"):
            if stage == "qkvd":
                pump_decode(len(DEC_TILES))
            for _d in ("do0", "do1", "do2", "do3"):
                B.wait("sync", _d)
            B.op("sync", lambda e: e.dma_start(out=dbg["qkv"][:, :], in_=qkv_sc[:, :]), inc="dz", k=16)
            _finish(B, block)
            return nc

        if stage != "decode":
            qkvc = [view(r3, 0, BF16, 6144), view(r3, 12288, BF16, 6144)]
            for _d in ("do0", "do1", "do2", "do3"):
                B.wait("sync", _d)
            for c in range(2):
                B.op("sync", lambda e, c=c: e.dma_start(out=qkvc[c % 2][:], in_=qkv_sc[c * 128:(c + 1) * 128, :]),
                     inc=f"dq{c % 2}", k=16)
            B.wait("vector", "sM")
            _is_bisect = stage.startswith("attn") or (stage.startswith("a") and "h" in stage and stage[1].isdigit())
            hbuf = [
                dict(qdb=qdb_tok, kdbb=kdb[:], qT=qT_sb, kT=kT_sb, qdT=qdT_sb[:], sc=scTd[:], St=Stmp,
                     ptr=pf(2), kq=pf(4)[:, 0:128], o=pf(5)[:, 0:128], S=pf(3)[:, 0:128]),
                dict(qdb=view(r4, 0, BF16, 128), kdbb=view(r4, 512, BF16, 128), qT=view(r4, 1024, BF16, 128),
                     kT=view(r4, 1536, BF16, 128), qdT=view(r4, 2048, BF16, 128), sc=view(r4, 2560, BF16, 128),
                     St=view(r4, 3072, BF16, 128),
                     ptr=pf(0), kq=pf(4)[:, 128:256], o=pf(5)[:, 128:256], S=pf(3)[:, 128:256]),
            ]
            mm_done = [0, 0]
            ev_done = [0, 0]
            sc_done = [0, 0]
            oev = [0, 0]
            su = [0, 0]
            for c in range(n_attn_chunks if _is_bisect else NCH):
                B.wait("scalar", f"dq{c % 2}")
                B.wait("tensor", f"dq{c % 2}")
                qc = qkvc[c % 2]
                for h in range(n_attn_heads if _is_bisect else H):
                    par = h % 2
                    hb = hbuf[par]
                    q_tok = qc[:, h * 128:(h + 1) * 128]
                    k_tok = qc[:, D + h * 128: D + (h + 1) * 128]
                    v_tok = qc[:, 2 * D + h * 128: 2 * D + (h + 1) * 128]
                    if mm_done[par]:
                        B.wait("scalar", "tM", mm_done[par])
                    B.op("scalar", lambda e, hb=hb, q_tok=q_tok, h=h: e.activation(
                        out=hb["qdb"], in_=q_tok, func=AF.Copy, scale=qd_col[:, h:h + 1]))
                    B.op("scalar", lambda e, hb=hb, k_tok=k_tok, h=h: e.activation(
                        out=hb["kdbb"], in_=k_tok, func=AF.Copy, scale=kd_col[:, h:h + 1]), inc="sM")
                    prep_s = B.sems["sM"].n
                    B.wait("tensor", "sM", prep_s)
                    B.op("tensor", lambda e, hb=hb, q_tok=q_tok: e.matmul(hb["ptr"][:, 0:128], q_tok, idn[:],
                                                                          start=True, stop=True))
                    B.op("tensor", lambda e, hb=hb, k_tok=k_tok: e.matmul(hb["ptr"][:, 128:256], k_tok, idn[:],
                                                                          start=True, stop=True))
                    B.op("tensor", lambda e, hb=hb: e.matmul(hb["ptr"][:, 256:384], hb["qdb"], idn[:],
                                                             start=True, stop=True))
                    tp_done = B.op("tensor", lambda e: e.drain(), inc="tM")
                    B.wait("scalar", "tM", tp_done)
                    B.op("scalar", lambda e, hb=hb: e.copy(out=hb["qT"], in_=hb["ptr"][:, 0:128]))
                    B.op("scalar", lambda e, hb=hb: e.copy(out=hb["kT"], in_=hb["ptr"][:, 128:256]))
                    ev_done[par] = B.op("scalar", lambda e, hb=hb: e.copy(out=hb["qdT"], in_=hb["ptr"][:, 256:384]),
                                        inc="sM")
                    B.wait("tensor", "sM", ev_done[par])
                    if sc_done[par]:
                        B.wait("tensor", "vM", sc_done[par])
                    B.op("tensor", lambda e, hb=hb: e.matmul(hb["kq"], hb["kT"], hb["qT"], start=True, stop=True))
                    kq_done = B.op("tensor", lambda e: e.drain(), inc="tM")
                    B.wait("vector", "tM", kq_done)
                    sc_done[par] = B.op("vector", lambda e, hb=hb, h=h: e.tensor_tensor(
                        out=hb["sc"], in0=hb["kq"], in1=decayT[:, h * 128:(h + 1) * 128], op=ALU.mult), inc="vM")
                    B.wait("tensor", "vM", sc_done[par])
                    if oev[par]:
                        B.wait("tensor", "sM", oev[par])
                    if su[par]:
                        B.wait("tensor", "vM", su[par])
                    B.op("tensor", lambda e, hb=hb, v_tok=v_tok, c=c: e.matmul(
                        hb["o"], v_tok, hb["sc"], start=True, stop=(c == 0)))
                    if c > 0:
                        B.op("tensor", lambda e, hb=hb, h=h: e.matmul(
                            hb["o"], S_bf[:, h * 128:(h + 1) * 128], hb["qdT"], start=False, stop=True))
                    B.op("tensor", lambda e, hb=hb, v_tok=v_tok: e.matmul(
                        hb["S"], hb["kdbb"], v_tok, start=True, stop=True))
                    mm_done[par] = B.op("tensor", lambda e: e.drain(), inc="tM")
                    B.wait("scalar", "tM", mm_done[par])
                    oev[par] = B.op("scalar", lambda e, hb=hb, h=h, c=c: e.copy(
                        out=aT[:, h * NTOK + c * 128: h * NTOK + (c + 1) * 128], in_=hb["o"]), inc="sM")
                    B.wait("vector", "tM", mm_done[par])
                    if c == 0:
                        su[par] = B.op("vector", lambda e, hb=hb, h=h: e.tensor_copy(
                            out=S_bf[:, h * 128:(h + 1) * 128], in_=hb["S"]), inc="vM")
                    else:
                        bdc = bd_col[:, h:h + 1]
                        bd_bc0 = bass.AP(bdc.tensor, bdc.offset, [list(bdc.ap[0]), [0, 128]])
                        B.op("vector", lambda e, hb=hb, h=h, bd_bc0=bd_bc0: e.tensor_tensor(
                            out=hb["St"], in0=S_bf[:, h * 128:(h + 1) * 128], in1=bd_bc0, op=ALU.mult))
                        su[par] = B.op("vector", lambda e, hb=hb, h=h: e.tensor_tensor(
                            out=S_bf[:, h * 128:(h + 1) * 128], in0=hb["St"], in1=hb["S"], op=ALU.add), inc="vM")
                if c + 2 < NCH:
                    B.wait("sync", "tM")
                    B.wait("sync", "sM")
                    B.op("sync", lambda e, c=c: e.dma_start(
                        out=qkvc[c % 2][:], in_=qkv_sc[(c + 2) * 128:(c + 3) * 128, :]), inc=f"dq{c % 2}", k=16)
                pump_decode(9)

        if stage.startswith("attn") or stage == "atd" or (stage.startswith("a") and "h" in stage and stage[1].isdigit()):
            if stage == "atd":
                pump_decode(len(DEC_TILES))
            B.wait("sync", "sM")
            B.op("sync", lambda e: e.dma_start(out=dbg["attn"][:, :], in_=aT[:]), inc="dz", k=16)
            _finish(B, block)
            return nc

        if stage != "decode":
            wo_pan = [view(r3, 0, BF16, 2048), view(r3, 4096, BF16, 2048)]
            tbuf = view(r4, 1024, BF16, 512)
            B.wait("tensor", "sM")
            B.wait("tensor", "vM")
            attn_done = B.sems["tM"].n
            pan_cons = [attn_done, attn_done]
            for cc in range(16):
                par = cc % 2
                B.wait("gpsimd", "tM", pan_cons[par])
                B.op("gpsimd", lambda e, cc=cc, par=par: e.dma_start(
                    out=wo_pan[par][:].rearrange("p (a b) -> p a b", b=128),
                    in_=bass.AP(wout.tensor, wout.offset + cc * 128, [[D, 128], [128 * D, 16], [1, 128]])),
                    inc=f"ds{par}", k=16)
                B.wait("tensor", f"ds{par}")
                for tg in range(4):
                    qp = tg % 2
                    B.wait("tensor", "vM")
                    for ac in range(16):
                        B.op("tensor", lambda e, ac=ac, tg=tg, par=par, qp=qp: e.matmul(
                            pf(qp)[:, 0:512],
                            wo_pan[par][:, ac * 128:(ac + 1) * 128],
                            aT[:, ac * NTOK + tg * 512: ac * NTOK + (tg + 1) * 512],
                            start=(ac == 0), stop=(ac == 15)), inc=("tM" if ac == 15 else None))
                    B.wait("vector", "tM")
                    B.op("vector", lambda e, cc=cc, qp=qp: e.tensor_scalar(
                        out=tbuf, in0=pf(qp)[:, 0:512], scalar1=bout_col[:, cc:cc + 1], scalar2=None, op0=ALU.add))
                    B.op("vector", lambda e, cc=cc, tg=tg: e.tensor_tensor(
                        out=xT[:, cc * NTOK + tg * 512: cc * NTOK + (tg + 1) * 512],
                        in0=tbuf,
                        in1=xT[:, cc * NTOK + tg * 512: cc * NTOK + (tg + 1) * 512], op=ALU.add), inc="vM")
                pan_cons[par] = B.sems["tM"].n
                pump_decode(6)
            a4_done = B.sems["tM"].n

            if stage == "a4d":
                pump_decode(len(DEC_TILES))
                B.wait("sync", "vM")
                B.op("sync", lambda e: e.dma_start(out=dbg["x1"][:, :], in_=xT[:]), inc="dz", k=16)
                _finish(B, block)
                return nc
            _emit_ln(B, nc, xT, NTOK, view, r4, PB, pf, on128, pump=pump_decode)


        if stage in ("ln1", "ln1d"):
            if stage == "ln1d":
                pump_decode(len(DEC_TILES))
            B.wait("sync", "vM")
            B.op("sync", lambda e: e.dma_start(out=dbg["x1"][:, :], in_=xT[:]), inc="dz", k=16)
            _finish(B, block)
            return nc

        if stage == "decode":
            pump_decode(len(DEC_TILES))
            for _d in ("dwg0", "dwg1", "dwu0", "dwu1", "dwd0", "dwd1"):
                B.wait("sync", _d)
            for nm, tgt, src in (("wg", dbg["wg"], wgT), ("wu", dbg["wu"], wuT), ("wd", dbg["wd"], wdT)):
                B.op("sync", lambda e, tgt=tgt, src=src: e.dma_start(out=tgt[:, :], in_=src[:, :]), inc="dz", k=16)
            _finish(B, block)
            return nc

        pump_decode(len(DEC_TILES))

        g_sl = [view(r3, 0, BF16, 2048), view(r3, 4096, BF16, 2048)]
        u_sl = [view(r3, 8192, BF16, 2048), view(r3, 12288, BF16, 2048)]
        x2pre = view(r4, 0, BF16, 8192)
        g2b = view(r3, 16384, BF16, 512)
        bb = view(r3, 17408, BF16, 512)
        thb = view(r3, 18432, BF16, 512)
        hgb = view(r3, 19456, BF16, 512)
        p1b = view(r3, 20480, BF16, 512)
        gelb = view(r3, 21504, BF16, 512)
        gsb2 = view(r3, 22528, BF16, 512)
        usb2 = view(r3, 23552, BF16, 512)
        dn_ev = [view(r3, 16384 + i * 1024, BF16, 512) for i in range(4)]
        mb2 = view(r5, 9216, F32, 512)
        rb2 = view(r5, 11264, F32, 512)
        t1b = view(r5, 13312, F32, 512)
        sqb2 = [view(r5, 15360, BF16, 512), view(r5, 16384, BF16, 512)]
        nbuf = view(r5, 17408, BF16, 512)
        otok = view(r5, 18944, F32, 1024)

        slab_cons = {"g": [a4_done, a4_done], "u": [a4_done, a4_done]}
        wd_cons = [0, 0, 0]
        for _d in ("dwg0", "dwg1", "dwu0", "dwu1"):
            B.wait("sync", _d)

        for tq in range(1 if stage == "ffn0" else 4):
            t0 = tq * 512
            for nh in range(2):
                B.wait("tensor", "sM")
                B.wait("tensor", "vM")
                psfree_s = {}
                psfree_v = {}
                for ncc in range(FFN_NCC):
                    n_abs = nh * 64 + ncc
                    spar = ncc % 2
                    B.wait("sync", "tM", slab_cons["g"][spar])
                    B.op("sync", lambda e, n_abs=n_abs, spar=spar: e.dma_start(
                        out=g_sl[spar][:].rearrange("p (a b) -> p a b", b=128),
                        in_=bass.AP(wgT.tensor, wgT.offset + n_abs * 128,
                                    [[MLP, 128], [128 * MLP, 16], [1, 128]])), inc=f"dg{spar}", k=16)
                    B.wait("sync", "tM", slab_cons["u"][spar])
                    if "u" in B_LOADS:
                        B.op("sync", lambda e, n_abs=n_abs, spar=spar: e.dma_start(
                            out=u_sl[spar][:].rearrange("p (a b) -> p a b", b=128),
                            in_=bass.AP(wuT.tensor, wuT.offset + n_abs * 128,
                                        [[MLP, 128], [128 * MLP, 16], [1, 128]])), inc=f"du{spar}", k=16)
                    B.wait("tensor", f"dg{spar}")
                    B.wait("tensor", f"du{spar}")
                    gq = 0 if ncc % 2 == 0 else 2
                    uq = 1 if ncc % 2 == 0 else 3
                    if psfree_s.get(gq):
                        B.wait("tensor", "sM", psfree_s[gq])
                    if psfree_v.get(uq):
                        B.wait("tensor", "vM", psfree_v[uq])
                    for kc in range(16):
                        B.op("tensor", lambda e, kc=kc, spar=spar, t0=t0, gq=gq: e.matmul(
                            pf(gq)[:, 0:512], g_sl[spar][:, kc * 128:(kc + 1) * 128],
                            xT[:, kc * NTOK + t0: kc * NTOK + t0 + 512],
                            start=(kc == 0), stop=(kc == 15)), inc=("tM" if kc == 15 else None))
                    for kc in range(16):
                        B.op("tensor", lambda e, kc=kc, spar=spar, t0=t0, uq=uq: e.matmul(
                            pf(uq)[:, 0:512], u_sl[spar][:, kc * 128:(kc + 1) * 128],
                            xT[:, kc * NTOK + t0: kc * NTOK + t0 + 512],
                            start=(kc == 0), stop=(kc == 15)), inc=("tM" if kc == 15 else None))
                    slab_cons["g"][spar] = B.sems["tM"].n
                    slab_cons["u"][spar] = B.sems["tM"].n
                    mm_tgt = B.sems["tM"].n
                    B.wait("scalar", "tM", mm_tgt)
                    B.wait("scalar", "vM")
                    B.wait("scalar", "gM")
                    B.op("scalar", lambda e, gq=gq: e.copy(out=gsb2, in_=pf(gq)[:, 0:512]))
                    B.op("scalar", lambda e: e.activation(out=g2b, in_=gsb2, func=AF.Square), inc="sM")
                    s1 = B.sems["sM"].n
                    psfree_s[gq] = s1
                    B.wait("vector", "sM", s1)
                    B.op("vector", lambda e: e.tensor_scalar(out=g2b, in0=g2b, scalar1=GELU_B, scalar2=GELU_A,
                                                             op0=ALU.mult, op1=ALU.add))
                    B.op("vector", lambda e: e.tensor_mul(out=bb, in0=g2b, in1=gsb2))
                    B.op("vector", lambda e: e.tensor_scalar(out=hgb, in0=gsb2, scalar1=0.5, scalar2=None,
                                                             op0=ALU.mult), inc="vM")
                    v1 = B.sems["vM"].n
                    B.wait("scalar", "vM", v1)
                    B.op("scalar", lambda e: e.activation(out=thb, in_=bb, func=AF.Tanh), inc="sM")
                    s2 = B.sems["sM"].n
                    B.op("vector", lambda e, uq=uq: e.tensor_copy(out=usb2, in_=pf(uq)[:, 0:512]))
                    B.wait("vector", "sM", s2)
                    B.op("vector", lambda e: e.tensor_mul(out=p1b, in0=hgb, in1=thb))
                    B.op("vector", lambda e: e.tensor_add(out=gelb, in0=p1b, in1=hgb))
                    B.wait("vector", "tM")
                    psfree_v[uq] = B.op("vector", lambda e, ncc=ncc: e.tensor_mul(
                        out=aT[:, ncc * 512:(ncc + 1) * 512], in0=gelb, in1=usb2), inc="vM")
                if FFN_STOP == "gelu" and nh == 0:
                    break
                B.wait("tensor", "sM")
                B.wait("tensor", "vM")
                B.wait("sync", "dwd0")
                B.wait("sync", "dwd1")
                dn_free = [0, 0]
                for gen in range(4):
                    bgrp = (gen % 2) * 4
                    if dn_free[gen % 2]:
                        B.wait("tensor", "sM", dn_free[gen % 2])
                    for pr in range(32):
                        slot = pr % 3
                        B.wait("sync", "tM", wd_cons[slot])
                        B.op("sync", lambda e, pr=pr, nh=nh, gen=gen, slot=slot: e.dma_start(
                            out=wd_sl[:, slot * 1024:(slot + 1) * 1024].rearrange("p (a b) -> p a b", b=512),
                            in_=bass.AP(wdT.tensor, wdT.offset + (nh * 64 + pr * 2) * 128 * D + gen * 512,
                                        [[D, 128], [128 * D, 2], [1, 512]])), inc=f"dd{slot}", k=16)
                        B.wait("tensor", f"dd{slot}")
                        for a in range(2):
                            ncc = pr * 2 + a
                            for hc in range(4):
                                B.op("tensor", lambda e, a=a, hc=hc, ncc=ncc, slot=slot, bgrp=bgrp: e.matmul(
                                    pf(bgrp + hc)[:, 0:512],
                                    wd_sl[:, slot * 1024 + a * 512 + hc * 128: slot * 1024 + a * 512 + (hc + 1) * 128],
                                    aT[:, ncc * 512:(ncc + 1) * 512],
                                    start=(ncc == 0), stop=(ncc == 63)), inc=("tM" if hc == 3 else None))
                        wd_cons[slot] = B.sems["tM"].n
                    gen_tgt = B.sems["tM"].n
                    B.wait("scalar", "tM", gen_tgt)
                    B.wait("scalar", "vM")
                    for hc in range(4):
                        B.op("scalar", lambda e, hc=hc, bgrp=bgrp: e.copy(
                            out=dn_ev[hc], in_=pf(bgrp + hc)[:, 0:512]), inc=("sM" if hc == 3 else None))
                    dn_free[gen % 2] = B.sems["sM"].n
                    B.wait("vector", "sM", dn_free[gen % 2])
                    for hc in range(4):
                        ct = gen * 4 + hc
                        if nh == 0:
                            B.op("vector", lambda e, ct=ct, hc=hc, t0=t0: e.tensor_tensor(
                                out=x2pre[:, ct * 512:(ct + 1) * 512],
                                in0=dn_ev[hc],
                                in1=xT[:, ct * NTOK + t0: ct * NTOK + t0 + 512], op=ALU.add),
                                inc=("vM" if hc == 3 else None))
                        else:
                            B.op("vector", lambda e, ct=ct, hc=hc: e.tensor_tensor(
                                out=x2pre[:, ct * 512:(ct + 1) * 512],
                                in0=dn_ev[hc],
                                in1=x2pre[:, ct * 512:(ct + 1) * 512], op=ALU.add),
                                inc=("vM" if hc == 3 else None))

                if FFN_STOP == "down0" and nh == 0:
                    break
            if stage == "ffn0":
                B.wait("sync", "vM")
                B.op("sync", lambda e: e.dma_start(out=dbg["x2"][:, :], in_=x2pre), inc="dz", k=16)
                B.op("sync", lambda e: e.dma_start(out=dbg["ih"][:, :], in_=aT[:]), inc="dz", k=16)
                _finish(B, block)
                return nc
            _emit_ln2_and_out(B, nc, x2pre, PB, pf, pbf, on128, idn,
                              mb2, rb2, t1b, sqb2, nbuf, otok, out, t0)

        _finish(B, block)
    return nc


def _emit_ln(B, nc, xT, ntok, view, r4, PB, pf, on128, pump=None):
    mb = view(r4, 0, F32, 512)
    rb = view(r4, 2048, F32, 512)
    t1 = view(r4, 4096, F32, 512)
    sqb = [view(r4, 6144, BF16, 512), view(r4, 7168, BF16, 512)]
    dbuf = view(r4, 8192, BF16, 512)
    B.wait("scalar", "vM")
    for tg in range(4):
        c0 = tg * 512
        for cc in range(16):
            sp = cc % 2
            B.wait("scalar", "tM")
            B.op("scalar", lambda e, cc=cc, c0=c0, sp=sp: e.activation(
                out=sqb[sp], in_=xT[:, cc * ntok + c0: cc * ntok + c0 + 512], func=AF.Square), inc="sM")
            B.wait("tensor", "sM")
            if cc == 0:
                B.wait("tensor", "vM")
            B.op("tensor", lambda e, cc=cc, c0=c0: e.matmul(
                pf(0)[:, 0:512], on128[:], xT[:, cc * ntok + c0: cc * ntok + c0 + 512],
                start=(cc == 0), stop=(cc == 15)))
            B.op("tensor", lambda e, cc=cc, sp=sp: e.matmul(
                pf(1)[:, 0:512], on128[:], sqb[sp],
                start=(cc == 0), stop=(cc == 15)), inc="tM")
        stats_tgt = B.sems["tM"].n
        B.wait("vector", "tM", stats_tgt)
        B.op("vector", lambda e: e.tensor_scalar(out=mb, in0=pf(0)[:, 0:512], scalar1=1.0 / 2048, scalar2=None,
                                                 op0=ALU.mult))
        B.op("vector", lambda e: e.tensor_scalar(out=t1, in0=pf(1)[:, 0:512], scalar1=1.0 / 2048, scalar2=None,
                                                 op0=ALU.mult))
        B.op("vector", lambda e: e.tensor_mul(out=rb, in0=mb, in1=mb))
        B.op("vector", lambda e: e.tensor_sub(out=t1, in0=t1, in1=rb))
        B.op("vector", lambda e: e.tensor_scalar(out=t1, in0=t1, scalar1=1e-5, scalar2=None, op0=ALU.add), inc="vM")
        B.wait("scalar", "vM")
        B.op("scalar", lambda e: e.activation(out=t1, in_=t1, func=AF.Ln))
        B.op("scalar", lambda e: e.activation(out=rb, in_=t1, func=AF.Exp, scale=-0.5), inc="sM")
        B.wait("vector", "sM")
        for cc in range(16):
            B.op("vector", lambda e, cc=cc, c0=c0: e.tensor_sub(
                out=dbuf, in0=xT[:, cc * ntok + c0: cc * ntok + c0 + 512], in1=mb))
            B.op("vector", lambda e, cc=cc, c0=c0: e.tensor_mul(
                out=xT[:, cc * ntok + c0: cc * ntok + c0 + 512], in0=dbuf, in1=rb),
                inc=("vM" if cc == 15 else None))
        if pump is not None:
            pump(6)


def _emit_ln2_and_out(B, nc, x2pre, PB, pf, pbf, on128, idn, mb, rb, t1, sqb, nbuf, otok, out, t0):
    for cc in range(16):
        sp = cc % 2
        B.wait("scalar", "tM")
        B.wait("scalar", "vM")
        B.op("scalar", lambda e, cc=cc, sp=sp: e.activation(
            out=sqb[sp], in_=x2pre[:, cc * 512:(cc + 1) * 512], func=AF.Square), inc="sM")
        B.wait("tensor", "sM")
        if cc == 0:
            B.wait("tensor", "vM")
        B.op("tensor", lambda e, cc=cc: e.matmul(
            pf(0)[:, 0:512], on128[:], x2pre[:, cc * 512:(cc + 1) * 512],
            start=(cc == 0), stop=(cc == 15)))
        B.op("tensor", lambda e, cc=cc, sp=sp: e.matmul(
            pf(1)[:, 0:512], on128[:], sqb[sp],
            start=(cc == 0), stop=(cc == 15)), inc="tM")
    stats_tgt = B.sems["tM"].n
    B.wait("vector", "tM", stats_tgt)
    B.op("vector", lambda e: e.tensor_scalar(out=mb, in0=pf(0)[:, 0:512], scalar1=1.0 / 2048, scalar2=None,
                                             op0=ALU.mult))
    B.op("vector", lambda e: e.tensor_scalar(out=t1, in0=pf(1)[:, 0:512], scalar1=1.0 / 2048, scalar2=None,
                                             op0=ALU.mult))
    B.op("vector", lambda e: e.tensor_mul(out=rb, in0=mb, in1=mb))
    B.op("vector", lambda e: e.tensor_sub(out=t1, in0=t1, in1=rb))
    B.op("vector", lambda e: e.tensor_scalar(out=t1, in0=t1, scalar1=1e-5, scalar2=None, op0=ALU.add), inc="vM")
    B.wait("scalar", "vM")
    B.op("scalar", lambda e: e.activation(out=t1, in_=t1, func=AF.Ln))
    B.op("scalar", lambda e: e.activation(out=rb, in_=t1, func=AF.Exp, scale=-0.5), inc="sM")
    for tsub in range(4):
        for half in range(2):
            for ci in range(8):
                cc = half * 8 + ci
                B.wait("vector", "sM")
                B.wait("vector", "tM")
                B.op("vector", lambda e, cc=cc: e.tensor_sub(
                    out=nbuf, in0=x2pre[:, cc * 512:(cc + 1) * 512], in1=mb))
                B.op("vector", lambda e: e.tensor_mul(out=nbuf, in0=nbuf, in1=rb), inc="vM")
                B.wait("tensor", "vM")
                B.op("tensor", lambda e, tsub=tsub: e.matmul(
                    pf(2)[:, 0:128], nbuf[:, tsub * 128:(tsub + 1) * 128], idn[:], start=True, stop=True))
                B.op("tensor", lambda e: e.drain(), inc="tM")
                B.wait("vector", "tM")
                B.wait("vector", "dz")
                B.op("vector", lambda e, ci=ci: e.tensor_copy(out=otok[:, ci * 128:(ci + 1) * 128],
                                                              in_=pf(2)[:, 0:128]), inc="vM")
            B.wait("sync", "vM")
            B.op("sync", lambda e, tsub=tsub, half=half, t0=t0: e.dma_start(
                out=out[t0 + tsub * 128: t0 + (tsub + 1) * 128, half * 1024:(half + 1) * 1024],
                in_=otok[:]), inc="dz", k=16)


def _finish(B, block):
    for _nm in list(B.sems):
        if _nm[0] == "d":
            B.wait("sync", _nm)
    for eng in ENGINES:
        thunks = B.prog[eng]
        dec = getattr(block, eng)

        def runner(e, thunks=thunks):
            for f in thunks:
                f(e)
        dec(runner)


_NC_CACHE = {}


def _get_nc(stage):
    if stage not in _NC_CACHE:
        _NC_CACHE[stage] = build(stage)
    return _NC_CACHE[stage]


def kernel(L0_attn_x, L0_attn_Wqkv, L0_attn_bqkv, L0_attn_Wout, L0_attn_bout, L0_attn_slopes,
           L0_ffn_gate_W_packed, L0_ffn_gate_scales, L0_ffn_up_W_packed, L0_ffn_up_scales,
           L0_ffn_down_W_packed, L0_ffn_down_scales):
    nc = _get_nc("full")
    x = np.ascontiguousarray(np.asarray(L0_attn_x, dtype=np.float32))
    common = {
        "wqkv": np.ascontiguousarray(np.asarray(L0_attn_Wqkv, dtype=np.float32)),
        "bqkv": np.ascontiguousarray(np.asarray(L0_attn_bqkv, dtype=np.float32)),
        "wout": np.ascontiguousarray(np.asarray(L0_attn_Wout, dtype=np.float32)),
        "bout": np.ascontiguousarray(np.asarray(L0_attn_bout, dtype=np.float32)),
        "slopes": np.ascontiguousarray(np.asarray(L0_attn_slopes, dtype=np.float32)),
        "g_pk": np.ascontiguousarray(np.asarray(L0_ffn_gate_W_packed, dtype=np.int32)),
        "g_sc": np.ascontiguousarray(np.asarray(L0_ffn_gate_scales, dtype=np.float32)),
        "u_pk": np.ascontiguousarray(np.asarray(L0_ffn_up_W_packed, dtype=np.int32)),
        "u_sc": np.ascontiguousarray(np.asarray(L0_ffn_up_scales, dtype=np.float32)),
        "d_pk": np.ascontiguousarray(np.asarray(L0_ffn_down_W_packed, dtype=np.int32)),
        "d_sc": np.ascontiguousarray(np.asarray(L0_ffn_down_scales, dtype=np.float32)),
    }
    in_maps = [dict(common, x=np.ascontiguousarray(x[b])) for b in range(8)]
    res = run_bass_kernel_spmd(nc, in_maps, core_ids=list(range(8)))
    return np.stack([r["out"] for r in res.results], axis=0)
